# version 20
# baseline (speedup 1.0000x reference)
import numpy as np
import ml_dtypes

import concourse.bass as bass
import concourse.mybir as mybir
from concourse.bass_utils import run_bass_kernel_spmd

F32 = mybir.dt.float32
BF16 = mybir.dt.bfloat16
I32 = mybir.dt.int32
ALU = mybir.AluOpType
ACT = mybir.ActivationFunctionType
AX = mybir.AxisListType

DEBUG_SIM = False

B, S, E, H, D = 2, 2048, 1024, 16, 64
EC = 8
TC = 4
CH = 512
NCORES = 8
MASK_AND = -262144

ENGINES = ("sync", "gpsimd", "vector", "scalar", "tensor")
DMA_FAMS = ("dx0", "dx1", "dx2", "dx3", "dwq00", "dwq01", "dwq10", "dwq11", "dwv",
            "dwf", "dsh", "dcst", "dout", "dqa0", "dqa1", "dka0", "dka1", "dmr0", "dmr1")
CHANNELS = ("TE", "SE", "VE", "GP", "SY") + DMA_FAMS
ENG2CHAN = {"tensor": "TE", "scalar": "SE", "vector": "VE", "gpsimd": "GP", "sync": "SY"}
COMPUTE_CHANS = ("TE", "SE", "VE", "GP", "SY")


class Buf:
    def __init__(self, handle, name, slot_cols):
        self.h = handle
        self.name = name
        self.slot = slot_cols

    def keys(self, c0=None, c1=None):
        if c0 is None:
            return [(self.name, -1)]
        return [(self.name, i) for i in range(c0 // self.slot, (c1 - 1) // self.slot + 1)]


class Op:
    __slots__ = ("eng", "chan", "fn", "idx", "eseq", "inc_k", "needs_inc",
                 "tok", "waits", "tag")

    def __init__(self, eng, chan, fn, idx, inc_k, tag):
        self.eng, self.chan, self.fn = eng, chan, fn
        self.idx, self.inc_k = idx, inc_k
        self.eseq = None
        self.needs_inc = False
        self.tok = None
        self.waits = {}
        self.tag = tag


class Sched:
    def __init__(self):
        self.ops = []
        self.last_w = {}
        self.readers = {}
        self.eng_seq = {e: 0 for e in ENGINES}
        self.last_drain = {e: -1 for e in ENGINES}
        self._idx = 0

    def _need_drain(self, op_eng, prod):
        return prod.eseq > self.last_drain[op_eng]

    def add(self, eng, fn, r=(), w=(), dma=None, tag=""):
        chan = dma if dma else ENG2CHAN[eng]
        rk = [k for grp in r for k in grp]
        wk = [k for grp in w for k in grp]
        self._idx += 1
        op = Op(eng, chan, fn, self._idx, 16 if dma else 1, tag)
        preds = []
        for k in rk:
            p = self.last_w.get(k)
            if p is not None:
                preds.append(p)
        for k in wk:
            p = self.last_w.get(k)
            if p is not None:
                preds.append(p)
            preds.extend(self.readers.get(k, ()))
        need_drain = False
        for p in preds:
            if p is op:
                continue
            if (dma is None) and p.chan == chan and chan in COMPUTE_CHANS:
                if DEBUG_SIM and eng != "tensor" and self._need_drain(eng, p):
                    need_drain = True
            else:
                cur = op.waits.get(p.chan)
                if cur is None or p.idx > cur.idx:
                    op.waits[p.chan] = p
                p.needs_inc = True
        if need_drain:
            self._idx += 1
            d = Op(eng, ENG2CHAN[eng], lambda e: e.drain(), self._idx - 1, 1, "drain")
            d.eseq = self.eng_seq[eng]
            self.eng_seq[eng] += 1
            self.last_drain[eng] = d.eseq
            self.ops.append(d)
        op.eseq = self.eng_seq[eng]
        self.eng_seq[eng] += 1
        self.ops.append(op)
        for k in rk:
            self.readers.setdefault(k, []).append(op)
        for k in wk:
            self.last_w[k] = op
            self.readers[k] = []
        return op

    def finalize(self):
        counts = {c: 0 for c in CHANNELS}
        for op in self.ops:
            if op.chan == "dout":
                op.needs_inc = True
            if op.needs_inc:
                counts[op.chan] += op.inc_k
                op.tok = counts[op.chan]
        per_eng = {e: [] for e in ENGINES}
        for op in self.ops:
            per_eng[op.eng].append(op)
        return counts, per_eng


def build_nc():
    nc = bass.Bass("TRN2", target_bir_lowering=False,
                   detect_race_conditions=DEBUG_SIM)

    xT = nc.dram_tensor("xT", [B, E, S], F32, kind="ExternalInput").ap()
    wqkvT = nc.dram_tensor("wqkvT", [E, 3 * E], F32, kind="ExternalInput").ap()
    wffnT_d = nc.dram_tensor("wffnT", [E, E], F32, kind="ExternalInput").ap()
    gcol_d = nc.dram_tensor("gcol", [128, EC], F32, kind="ExternalInput").ap()
    bcol_d = nc.dram_tensor("bcol", [128, EC], F32, kind="ExternalInput").ap()
    fbcol_d = nc.dram_tensor("fbcol", [128, EC], F32, kind="ExternalInput").ap()
    iscale_d = nc.dram_tensor("iscale", [1, 1], F32, kind="ExternalInput").ap()
    wscale_d = nc.dram_tensor("wscale", [1, 1], F32, kind="ExternalInput").ap()
    trimin_d = nc.dram_tensor("trimin", [128, 128], F32, kind="ExternalInput").ap()
    outT = nc.dram_tensor("outT", [B, E, S], F32, kind="ExternalOutput").ap()

    from contextlib import ExitStack
    es = ExitStack()

    def sb(name, shape, dt, slot_cols):
        return Buf(es.enter_context(nc.sbuf_tensor(name, shape, dt)), name, slot_cols)

    def ps(name, shape):
        return Buf(es.enter_context(nc.psum_tensor(name, shape, F32)), name, 1 << 30)

    z1T = [sb(f"z1T{i}", [128, S], BF16, CH) for i in range(EC)]
    attnS = [sb(f"attnS{i}", [128, S], BF16, CH) for i in range(EC)]
    vTa = sb("vTa", [128, 16 * 16 * 65], BF16, 1040)
    pTA = sb("pTA", [128, 16 * CH], BF16, CH)
    pTB = sb("pTB", [128, 16 * CH], BF16, CH)
    q2 = sb("q2", [128, S], BF16, CH)
    k2 = sb("k2", [128, S], BF16, CH)
    wqk = [sb(f"wqk{i}", [128, EC * 256], BF16, 1 << 30) for i in range(2)]
    xsub = [sb(f"xsub{i}", [128, CH], F32, 1 << 30) for i in range(4)]
    tmpB = sb("tmpB", [128, S], BF16, CH)
    qa = sb("qa", [128, 2 * S], BF16, CH)
    ka = sb("ka", [128, 2 * S], BF16, CH)
    murow = sb("murow", [1, CH], F32, 1 << 30)
    rsrow = sb("rsrow", [1, CH], F32, 1 << 30)
    kn2row = sb("kn2row", [2, S], BF16, 1 << 30)
    kpm2row = sb("kpm2row", [2, S], BF16, 1 << 30)
    mrowb = sb("mrowb", [2, CH], BF16, 1 << 30)
    ones = sb("ones", [128, 128], F32, 1 << 30)
    bd2 = sb("bd2", [128, 2], BF16, 1 << 30)
    gcol = sb("gcol_s", [128, EC], F32, 1)
    bcol = sb("bcol_s", [128, EC], F32, 1)
    fbcol = sb("fbcol_s", [128, EC], F32, 1)
    sc_is = sb("sc_is", [1, 1], F32, 1 << 30)
    sc_ws = sb("sc_ws", [1, 1], F32, 1 << 30)
    sc_ri = sb("sc_ri", [1, 1], F32, 1 << 30)
    sc_sws = sb("sc_sws", [1, 1], F32, 1 << 30)
    ri_col = sb("ri_col", [128, 1], F32, 1 << 30)
    sws_col = sb("sws_col", [128, 1], F32, 1 << 30)
    c448col = sb("c448col", [128, 1], F32, 1 << 30)
    m448col = sb("m448col", [128, 1], F32, 1 << 30)
    ri2col = sb("ri2col", [128, 1], F32, 1 << 30)
    lnricol = sb("lnricol", [128, 1], F32, 1 << 30)
    lniscol = sb("lniscol", [128, 1], F32, 1 << 30)
    wscol = sb("wscol", [128, 1], F32, 1 << 30)
    sc_c448 = sb("sc_c448", [1, 1], F32, 1 << 30)
    sc_m448 = sb("sc_m448", [1, 1], F32, 1 << 30)
    sc_ri2 = sb("sc_ri2", [1, 1], F32, 1 << 30)
    sc_lnri = sb("sc_lnri", [1, 1], F32, 1 << 30)
    sc_lnis = sb("sc_lnis", [1, 1], F32, 1 << 30)
    trimin = sb("trimin_s", [128, 128], F32, 1 << 30)
    rct = sb("rct", [128, CH], F32, 1 << 30)
    sc_t = sb("sc_t", [128, CH], F32, 1 << 30)
    sc_w = sb("sc_w", [128, CH], F32, 1 << 30)
    sc_c1 = sb("sc_c1", [128, CH], F32, 1 << 30)
    sc_n1 = sb("sc_n1", [128, CH], F32, 1 << 30)
    sc_r2 = sb("sc_r2", [128, CH], F32, 1 << 30)
    sc_sq = sb("sc_sq", [128, CH], F32, 1 << 30)
    sc_ff = sb("sc_ff", [128, CH], F32, 1 << 30)
    qsq = tmpB
    epscol = sb("epscol", [128, 1], F32, 1 << 30)

    class AliasBuf:
        __slots__ = ("h", "name", "_keys")
        def __init__(self, name, h, keys):
            self.name, self.h, self._keys = name, h, keys
        def keys(self, *a):
            return self._keys

    def alias_tile(nm, addr, under_keys):
        h = nc.virtual_sbuf_tensor(nm, [128, CH], F32)
        nc._tensor(h.name, h.shape, h.dtype, type="SB")
        mloc = nc.lookup_mloc(h)
        mloc.allocated = not nc.target_bir_lowering
        mloc.addr = addr
        return AliasBuf(nm, h, under_keys)

    _qa_addr = nc.lookup_mloc(qa.h).addr
    _ka_addr = nc.lookup_mloc(ka.h).addr
    sB_t = alias_tile("sB_t", _qa_addr + 0, qa.keys(0, 1024))
    sB_c1 = alias_tile("sB_c1", _qa_addr + 2048, qa.keys(1024, 2048))
    sB_n1 = alias_tile("sB_n1", _qa_addr + 4096, qa.keys(2048, 3072))
    sB_w = alias_tile("sB_w", _qa_addr + 6144, qa.keys(3072, 4096))
    sB_sq = alias_tile("sB_sq", _ka_addr + 0, ka.keys(0, 1024))
    sB_ff = alias_tile("sB_ff", _ka_addr + 2048, ka.keys(1024, 2048))
    sD_a = alias_tile("sD_a", _ka_addr + 4096, ka.keys(2048, 3072))
    sD_b = alias_tile("sD_b", _ka_addr + 6144, ka.keys(3072, 4096))
    _q2_addr = nc.lookup_mloc(q2.h).addr
    _k2_addr = nc.lookup_mloc(k2.h).addr
    _tmpB_addr = nc.lookup_mloc(tmpB.h).addr
    sC_t = alias_tile("sC_t", _q2_addr + 0, q2.keys(0, 1024))
    sC_c1 = alias_tile("sC_c1", _q2_addr + 2048, q2.keys(1024, 2048))
    sC_n1 = alias_tile("sC_n1", _k2_addr + 0, k2.keys(0, 1024))
    sC_w = alias_tile("sC_w", _k2_addr + 2048, k2.keys(1024, 2048))
    sC_sq = alias_tile("sC_sq", _tmpB_addr + 0, tmpB.keys(0, 1024))
    sC_ff = alias_tile("sC_ff", _tmpB_addr + 2048, tmpB.keys(1024, 2048))

    PW0 = ps("PW0", [128, 2 * CH])
    PW1 = ps("PW1", [128, 2 * CH])

    class _HalfH:
        __slots__ = ("th", "c0")
        def __init__(self, th, c0):
            self.th, self.c0 = th, c0
        def __getitem__(self, idx):
            if isinstance(idx, tuple):
                rows, cols = idx
            else:
                rows, cols = idx, slice(None)
            cs = cols.start or 0
            ce = cols.stop if cols.stop is not None else CH
            return self.th[rows, self.c0 + cs:self.c0 + ce]

    class _HalfBuf:
        __slots__ = ("h", "name")
        def __init__(self, pw, c0, name):
            self.h = _HalfH(pw.h, c0)
            self.name = name
        def keys(self, *a):
            return [(self.name, -1)]

    P0 = _HalfBuf(PW0, 0, "P0h")
    P1 = _HalfBuf(PW0, CH, "P1h")
    P2 = _HalfBuf(PW1, 0, "P2h")
    P3 = _HalfBuf(PW1, CH, "P3h")
    PAVA = ps("PAVA", [128, CH])
    PAVB = ps("PAVB", [128, CH])
    PBC = ps("PBC", [128, CH])
    PM = ps("PM", [128, CH])

    va3 = vTa.h[:].rearrange("p (t c) -> p t c", c=65)

    sch = Sched()
    A = sch.add

    A("sync", lambda e: e.dma_start(out=gcol.h[:], in_=gcol_d), w=[gcol.keys(0, EC), (("cstchain", 0),)], dma="dcst")
    A("sync", lambda e: e.dma_start(out=bcol.h[:], in_=bcol_d), w=[bcol.keys(0, EC), (("cstchain", 0),)], dma="dcst")
    A("sync", lambda e: e.dma_start(out=fbcol.h[:], in_=fbcol_d), w=[fbcol.keys(0, EC), (("cstchain", 0),)], dma="dcst")
    A("sync", lambda e: e.dma_start(out=sc_is.h[:], in_=iscale_d), w=[sc_is.keys(), (("cstchain", 0),)], dma="dcst")
    A("sync", lambda e: e.dma_start(out=sc_ws.h[:], in_=wscale_d), w=[sc_ws.keys(), (("cstchain", 0),)], dma="dcst")
    A("sync", lambda e: e.dma_start(out=trimin.h[:], in_=trimin_d), w=[trimin.keys(), (("cstchain", 0),)], dma="dcst")
    A("vector", lambda e: e.memset(ones.h[:], 1.0), w=[ones.keys()])
    A("vector", lambda e: e.memset(bd2.h[:], 0.0), w=[bd2.keys()])
    A("vector", lambda e: e.memset(bd2.h[0:64, 0:1], 1.0), w=[bd2.keys()])
    A("vector", lambda e: e.memset(bd2.h[64:128, 1:2], 1.0), w=[bd2.keys()])
    A("vector", lambda e: e.memset(epscol.h[:], 1e-38), w=[epscol.keys()])
    A("vector", lambda e: e.reciprocal(sc_ri.h[:], sc_is.h[:]), r=[sc_is.keys()], w=[sc_ri.keys()])
    A("vector", lambda e: e.tensor_tensor(sc_sws.h[:], sc_is.h[:], sc_ws.h[:], ALU.mult),
      r=[sc_is.keys(), sc_ws.keys()], w=[sc_sws.keys()])
    A("tensor", lambda e: e.matmul(PM.h[0:128, 0:1], ones.h[0:1, 0:128], sc_ri.h[0:1, 0:1],
                                   start=True, stop=True),
      r=[ones.keys(), sc_ri.keys()], w=[PM.keys()])
    A("vector", lambda e: e.tensor_copy(ri_col.h[:], PM.h[0:128, 0:1]),
      r=[PM.keys()], w=[ri_col.keys()])
    A("tensor", lambda e: e.matmul(PM.h[0:128, 0:1], ones.h[0:1, 0:128], sc_sws.h[0:1, 0:1],
                                   start=True, stop=True),
      r=[ones.keys(), sc_sws.keys()], w=[PM.keys()])
    A("vector", lambda e: e.tensor_copy(sws_col.h[:], PM.h[0:128, 0:1]),
      r=[PM.keys()], w=[sws_col.keys()])
    A("vector", lambda e: e.tensor_scalar(sc_c448.h[:], sc_is.h[:], 448.0, None, ALU.mult),
      r=[sc_is.keys()], w=[sc_c448.keys()])
    A("vector", lambda e: e.tensor_scalar(sc_m448.h[:], sc_is.h[:], -448.0, None, ALU.mult),
      r=[sc_is.keys()], w=[sc_m448.keys()])
    A("vector", lambda e: e.tensor_tensor(sc_ri2.h[:], sc_ri.h[:], sc_ri.h[:], ALU.mult),
      r=[sc_ri.keys()], w=[sc_ri2.keys()])
    A("scalar", lambda e: e.activation(sc_lnri.h[:], sc_ri.h[:], ACT.Ln),
      r=[sc_ri.keys()], w=[sc_lnri.keys()])
    A("scalar", lambda e: e.activation(sc_lnis.h[:], sc_is.h[:], ACT.Ln),
      r=[sc_is.keys()], w=[sc_lnis.keys()])
    for _src, _dst in ((sc_c448, c448col), (sc_m448, m448col), (sc_ri2, ri2col),
                       (sc_lnri, lnricol), (sc_lnis, lniscol), (sc_ws, wscol)):
        A("tensor", (lambda _src=_src: lambda e: e.matmul(
            PM.h[0:128, 0:1], ones.h[0:1, 0:128], _src.h[0:1, 0:1],
            start=True, stop=True))(),
          r=[ones.keys(), _src.keys()], w=[PM.keys()])
        A("vector", (lambda _dst=_dst: lambda e: e.tensor_copy(
            _dst.h[:], PM.h[0:128, 0:1]))(),
          r=[PM.keys()], w=[_dst.keys()])

    def x_dma(i, s, ec, tc):
        return lambda e: e.dma_start(
            out=xsub[i].h[:], in_=xT[s, ec * 128:(ec + 1) * 128, tc * CH:(tc + 1) * CH])

    def wqk_dma(buf, p, j):
        c0 = 128 * p if j == 0 else E + 128 * p
        def fn(e):
            src = wqkvT[:, c0:c0 + 128].rearrange("(c p) n -> p c n", p=128)
            dst = buf.h[:].rearrange("p (c n) -> p c n", n=256)[:, :, j * 128:(j + 1) * 128]
            return e.dma_start(out=dst, in_=src)
        return fn

    def wv_dma(e):
        src = wqkvT[:, 2 * E:3 * E].rearrange("(c p) n -> p c n", p=128)
        dst = pTA.h[:].rearrange("p (c n) -> p c n", n=1024)
        return e.dma_start(out=dst, in_=src)

    def wffn_dma(e):
        src = wffnT_d.rearrange("(c p) n -> p c n", p=128)
        dst = vTa.h[:, 0:EC * 1024].rearrange("p (c n) -> p c n", n=1024)
        return e.dma_start(out=dst, in_=src)

    def ln_tc(s, src_kind, tc):
        xi = (tc * EC) % 4
        for ec in range(EC):
            i = xi
            xk = (("xsub", i),)
            A("sync", x_dma(i, s, ec, tc), w=[xk], dma=f"dx{i}")
            if src_kind == "x":
                srcb, srck = xsub[i].h, [xk]
            else:
                A("vector", (lambda i=i, ec=ec, tc=tc: lambda e: e.tensor_tensor(
                    sc_r2.h[:], xsub[i].h[:], attnS[ec].h[:, tc * CH:(tc + 1) * CH],
                    ALU.add))(),
                  r=[xk, attnS[ec].keys(tc * CH, (tc + 1) * CH)], w=[sc_r2.keys()])
                srcb, srck = sc_r2.h, [sc_r2.keys()]
            A("scalar", (lambda srcb=srcb: lambda e: e.activation(
                rct.h[:], srcb[:], ACT.Square))(), r=srck, w=[rct.keys()])
            A("tensor", (lambda srcb=srcb, ec=ec: lambda e: e.matmul(
                P2.h[0:1, :], ones.h[0:128, 0:1], srcb[:],
                start=(ec == 0), stop=(ec == EC - 1)))(),
              r=srck + [ones.keys()], w=[P2.keys()])
            A("tensor", (lambda ec=ec: lambda e: e.matmul(
                P3.h[0:1, :], ones.h[0:128, 0:1], rct.h[:],
                start=(ec == 0), stop=(ec == EC - 1)))(),
              r=[rct.keys(), ones.keys()], w=[P3.keys()])
            xi = (xi + 1) % 4
        A("scalar", lambda e: e.activation(murow.h[:], P2.h[0:1, :], ACT.Copy, scale=1.0 / E),
          r=[P2.keys()], w=[murow.keys()])
        A("scalar", lambda e: e.activation(rsrow.h[:], P3.h[0:1, :], ACT.Copy, scale=1.0 / E),
          r=[P3.keys()], w=[rsrow.keys()])
        A("vector", lambda e: e.tensor_tensor(PM.h[0:1, :], murow.h[:], murow.h[:], ALU.mult),
          r=[murow.keys()], w=[PM.keys()])
        A("vector", lambda e: e.tensor_tensor(rsrow.h[:], rsrow.h[:], PM.h[0:1, :], ALU.subtract),
          r=[PM.keys(), rsrow.keys()], w=[rsrow.keys()])
        A("vector", lambda e: e.tensor_scalar(rsrow.h[:], rsrow.h[:], 1e-5, None, ALU.add),
          r=[rsrow.keys()], w=[rsrow.keys()])
        A("scalar", lambda e: e.activation(rsrow.h[:], rsrow.h[:], ACT.Ln),
          r=[rsrow.keys()], w=[rsrow.keys()])
        A("scalar", lambda e: e.activation(rsrow.h[:], rsrow.h[:], ACT.Exp, scale=-0.5),
          r=[rsrow.keys()], w=[rsrow.keys()])
        A("tensor", lambda e: e.matmul(P0.h[:], ones.h[0:1, 0:128], murow.h[:],
                                       start=True, stop=True),
          r=[ones.keys(), murow.keys()], w=[P0.keys()])
        A("tensor", lambda e: e.matmul(P1.h[:], ones.h[0:1, 0:128], rsrow.h[:],
                                       start=True, stop=True),
          r=[ones.keys(), rsrow.keys()], w=[P1.keys()])

    def phaseA(s):
        for tc in range(TC):
            ln_tc(s, "x", tc)
            c0, c1 = tc * CH, (tc + 1) * CH
            xi = (tc * EC) % 4
            for ec in range(EC):
                i = xi
                xk = (("xsub", i),)
                A("sync", x_dma(i, s, ec, tc), w=[xk], dma=f"dx{i}")
                A("vector", (lambda i=i: lambda e: e.tensor_tensor(
                    sc_t.h[:], xsub[i].h[:], P0.h[:], ALU.subtract))(),
                  r=[xk, P0.keys()], w=[sc_t.keys()])
                A("vector", lambda e: e.tensor_tensor(sc_w.h[:], sc_t.h[:], P1.h[:], ALU.mult),
                  r=[sc_t.keys(), P1.keys()], w=[sc_w.keys()])
                A("vector", (lambda ec=ec, c0=c0, c1=c1: lambda e: e.tensor_scalar(
                    z1T[ec].h[:, c0:c1], sc_w.h[:], gcol.h[:, ec:ec + 1],
                    bcol.h[:, ec:ec + 1], ALU.mult, ALU.add))(),
                  r=[sc_w.keys(), gcol.keys(0, EC), bcol.keys(0, EC)],
                  w=[z1T[ec].keys(c0, c1)])
                xi = (xi + 1) % 4

    def phaseB(s):
        A("gpsimd", wv_dma, w=[pTA.keys(0, 16 * CH)], dma="dwv")
        A("vector", lambda e: e.memset(va3[:, :, 64:65], 1.0),
          w=[vTa.keys(0, 16 * 16 * 65), (("vTa_ones", 0),)])
        for tt in range(16):
            for half in range(2):
                PT = P2 if half == 0 else P3
                for ec in range(EC):
                    A("tensor", (lambda tt=tt, half=half, ec=ec, PT=PT: lambda e: e.matmul(
                        PT.h[:], z1T[ec].h[:, tt * 128:(tt + 1) * 128],
                        pTA.h[:, ec * 1024 + half * CH:ec * 1024 + (half + 1) * CH],
                        start=(ec == 0), stop=(ec == EC - 1)))(),
                      r=[z1T[ec].keys(tt * 128, (tt + 1) * 128),
                         pTA.keys(ec * 1024 + half * CH, ec * 1024 + (half + 1) * CH)],
                      w=[PT.keys()])
                h0 = half * 8
                A("vector", (lambda tt=tt, h0=h0, PT=PT: lambda e: e.tensor_copy(
                    va3[:, tt * 16 + h0:tt * 16 + h0 + 8, 0:64],
                    PT.h[:].rearrange("p (h d) -> p h d", d=64)))(),
                  r=[PT.keys(), (("vTa_ones", 0),)],
                  w=[vTa.keys(tt * 1040, (tt + 1) * 1040)])

    def phaseB2_pair(s, p, wbuf):
        for tc in range(TC):
            phaseB2_tc(s, p, wbuf, tc)

    def phaseB2_tc(s, p, wbuf, tc):
        if True:
            c0, c1 = tc * CH, (tc + 1) * CH
            PQ = P0 if tc % 2 == 0 else P1
            PK = P2 if tc % 2 == 0 else P3
            for ec in range(EC):
                A("tensor", (lambda ec=ec, c0=c0, c1=c1, PQ=PQ, wbuf=wbuf: lambda e: e.matmul(
                    PQ.h[:], wqk[wbuf].h[:, ec * 256:ec * 256 + 128], z1T[ec].h[:, c0:c1],
                    start=(ec == 0), stop=(ec == EC - 1)))(),
                  r=[(("wqkh", wbuf, 0), ("wqkh", wbuf, 1)), z1T[ec].keys(c0, c1)], w=[PQ.keys()])
            for ec in range(EC):
                A("tensor", (lambda ec=ec, c0=c0, c1=c1, PK=PK, wbuf=wbuf: lambda e: e.matmul(
                    PK.h[:], wqk[wbuf].h[:, ec * 256 + 128:ec * 256 + 256],
                    z1T[ec].h[:, c0:c1],
                    start=(ec == 0), stop=(ec == EC - 1)))(),
                  r=[(("wqkh", wbuf, 0), ("wqkh", wbuf, 1)), z1T[ec].keys(c0, c1)], w=[PK.keys()])
            A("scalar", (lambda c0=c0, c1=c1, PQ=PQ: lambda e: e.activation(
                q2.h[:, c0:c1], PQ.h[:], ACT.Copy, scale=0.125))(),
              r=[PQ.keys()], w=[q2.keys(c0, c1)])
            A("scalar", (lambda c0=c0, c1=c1, PK=PK: lambda e: e.activation(
                k2.h[:, c0:c1], PK.h[:], ACT.Copy))(),
              r=[PK.keys()], w=[k2.keys(c0, c1)])

    def stage_dmas(s, p):
        A("sync", lambda e: e.dma_start(out=qa.h[0:64, 0:S], in_=q2.h[0:64, :]),
          r=[q2.keys(0, S)], w=[qa.keys(0, S)], dma="dqa0")
        A("sync", lambda e: e.dma_start(out=qa.h[0:64, S:2 * S], in_=q2.h[64:128, :]),
          r=[q2.keys(0, S)], w=[qa.keys(S, 2 * S)], dma="dqa1")
        A("sync", lambda e: e.dma_start(out=ka.h[0:64, 0:S], in_=k2.h[0:64, :]),
          r=[k2.keys(0, S)], w=[ka.keys(0, S)], dma="dka0")
        A("sync", lambda e: e.dma_start(out=ka.h[0:64, S:2 * S], in_=k2.h[64:128, :]),
          r=[k2.keys(0, S)], w=[ka.keys(S, 2 * S)], dma="dka1")

    def stage_norms(s, p):
        A("gpsimd", lambda e: e.tensor_tensor(qsq.h[:], k2.h[:], k2.h[:], ALU.mult),
          r=[k2.keys(0, S)], w=[qsq.keys(0, S)])
        for tc in range(TC):
            c0, c1 = tc * CH, (tc + 1) * CH
            A("tensor", (lambda c0=c0, c1=c1: lambda e: e.matmul(
                PM.h[0:2, :], bd2.h[:], qsq.h[:, c0:c1], start=True, stop=True))(),
              r=[bd2.keys(), qsq.keys(c0, c1)], w=[PM.keys()])
            A("vector", (lambda c0=c0: lambda e: e.tensor_copy(
                kn2row.h[0:2, c0:c0 + CH], PM.h[0:2, :]))(),
              r=[PM.keys()], w=[(("kn2", c0),)])
        A("vector", lambda e: e.tensor_tensor_scan(
            kpm2row.h[0:2, :], kn2row.h[0:2, :], kn2row.h[0:2, :], 0.0, ALU.max, ALU.max),
          r=[tuple(("kn2", c) for c in range(0, S, CH))], w=[(("kpm", 0),)])
        A("vector", lambda e: e.tensor_scalar(kpm2row.h[0:2, :], kpm2row.h[0:2, :],
                                              1.004, None, ALU.mult),
          r=[(("kpm", 0),)], w=[(("kpm", 0),)])
        A("gpsimd", lambda e: e.tensor_tensor(qsq.h[:], q2.h[:], q2.h[:], ALU.mult),
          r=[q2.keys(0, S)], w=[qsq.keys(0, S)])
        for tc in range(TC):
            c0, c1 = tc * CH, (tc + 1) * CH
            A("tensor", (lambda c0=c0, c1=c1: lambda e: e.matmul(
                PM.h[0:2, :], bd2.h[:], qsq.h[:, c0:c1], start=True, stop=True))(),
              r=[bd2.keys(), qsq.keys(c0, c1)], w=[PM.keys()])
            A("vector", (lambda c0=c0: lambda e: e.tensor_tensor(
                rct.h[0:2, :], PM.h[0:2, :], kpm2row.h[0:2, c0:c0 + CH], ALU.mult))(),
              r=[PM.keys(), (("kpm", 0),)], w=[rct.keys()])
            A("vector", lambda e: e.tensor_scalar(rct.h[0:2, :], rct.h[0:2, :],
                                                  1e-35, None, ALU.add),
              r=[rct.keys()], w=[rct.keys()])
            A("scalar", lambda e: e.activation(rct.h[0:2, :], rct.h[0:2, :], ACT.Ln),
              r=[rct.keys()], w=[rct.keys()])
            A("scalar", lambda e: e.activation(mrowb.h[:], rct.h[0:2, :], ACT.Exp, scale=0.5),
              r=[rct.keys()], w=[mrowb.keys()])
            A("sync", (lambda c0=c0: lambda e: e.dma_start(
                out=qa.h[64:65, c0:c0 + CH], in_=mrowb.h[0:1, :]))(),
              r=[mrowb.keys()], w=[(("qamr", 0, c0 // CH),)], dma="dmr0")
            A("sync", (lambda c0=c0: lambda e: e.dma_start(
                out=qa.h[64:65, S + c0:S + c0 + CH], in_=mrowb.h[1:2, :]))(),
              r=[mrowb.keys()], w=[(("qamr", 1, c0 // CH),)], dma="dmr1")

    def phaseC_pair(s, p, hook=None):
        hA, hB = 2 * p, 2 * p + 1
        for qc in range(TC):
            q0, q1 = qc * CH, (qc + 1) * CH
            nkt = 4 * qc + 4
            def sc_mm(head, kt, PH):
                off = 0 if head == 0 else S
                mr = (("qamr", head, qc),)
                A("tensor", (lambda kt=kt, q0=q0, q1=q1, off=off, PH=PH, mr=mr: lambda e: e.matmul(
                    PH.h[:], ka.h[0:65, off + kt * 128:off + (kt + 1) * 128],
                    qa.h[0:65, off + q0:off + q1], start=True, stop=True))(),
                  r=[ka.keys(off + kt * 128, off + (kt + 1) * 128), (("ka_neg1", 0),),
                     qa.keys(off + q0, off + q1), mr], w=[PH.keys()])

            nplain = 4 * qc
            for kt in range(nplain):
                PA = P0 if kt % 2 == 0 else P1
                PB = P2 if kt % 2 == 0 else P3
                sc_mm(0, kt, PA)
                sc_mm(1, kt, PB)
                A("scalar", (lambda kt=kt, PA=PA: lambda e: e.activation(
                    pTA.h[:, kt * CH:(kt + 1) * CH], PA.h[:], ACT.Exp))(),
                  r=[PA.keys()], w=[pTA.keys(kt * CH, (kt + 1) * CH)])
                A("scalar", (lambda kt=kt, PB=PB: lambda e: e.activation(
                    pTB.h[:, kt * CH:(kt + 1) * CH], PB.h[:], ACT.Exp))(),
                  r=[PB.keys()], w=[pTB.keys(kt * CH, (kt + 1) * CH)])
            for kt in range(nplain, nkt):
                PA = P0 if kt % 2 == 0 else P1
                PB = P2 if kt % 2 == 0 else P3
                jo = 128 * (kt - 4 * qc)
                sc_mm(0, kt, PA)
                sc_mm(1, kt, PB)
                A("vector", (lambda jo=jo, PA=PA: lambda e: e.tensor_tensor(
                    PA.h[:, jo:jo + 128], PA.h[:, jo:jo + 128], trimin.h[:], ALU.min))(),
                  r=[PA.keys(), trimin.keys()], w=[PA.keys()])
                A("vector", (lambda jo=jo, PB=PB: lambda e: e.tensor_tensor(
                    PB.h[:, jo:jo + 128], PB.h[:, jo:jo + 128], trimin.h[:], ALU.min))(),
                  r=[PB.keys(), trimin.keys()], w=[PB.keys()])
                A("scalar", (lambda kt=kt, jo=jo, PA=PA: lambda e: e.activation(
                    pTA.h[:, kt * CH + jo:(kt + 1) * CH], PA.h[:, jo:], ACT.Exp))(),
                  r=[PA.keys()], w=[pTA.keys(kt * CH, (kt + 1) * CH)])
                A("scalar", (lambda kt=kt, jo=jo, PB=PB: lambda e: e.activation(
                    pTB.h[:, kt * CH + jo:(kt + 1) * CH], PB.h[:, jo:], ACT.Exp))(),
                  r=[PB.keys()], w=[pTB.keys(kt * CH, (kt + 1) * CH)])
                if jo > 0:
                    A("gpsimd", (lambda kt=kt, jo=jo: lambda e: e.memset(
                        pTA.h[:, kt * CH:kt * CH + jo], 0.0))(),
                      w=[pTA.keys(kt * CH, (kt + 1) * CH)])
                    A("gpsimd", (lambda kt=kt, jo=jo: lambda e: e.memset(
                        pTB.h[:, kt * CH:kt * CH + jo], 0.0))(),
                      w=[pTB.keys(kt * CH, (kt + 1) * CH)])
            for kt in range(nkt):
                A("tensor", (lambda kt=kt, hA=hA, nkt=nkt: lambda e: e.matmul(
                    PAVA.h[0:65, :], vTa.h[:, (kt * 16 + hA) * 65:(kt * 16 + hA) * 65 + 65],
                    pTA.h[:, kt * CH:(kt + 1) * CH],
                    start=(kt == 0), stop=(kt == nkt - 1)))(),
                  r=[vTa.keys(kt * 1040, (kt + 1) * 1040), (("vTa_ones", 0),),
                     pTA.keys(kt * CH, (kt + 1) * CH)],
                  w=[PAVA.keys()])
            for kt in range(nkt):
                A("tensor", (lambda kt=kt, hB=hB, nkt=nkt: lambda e: e.matmul(
                    PAVB.h[0:65, :], vTa.h[:, (kt * 16 + hB) * 65:(kt * 16 + hB) * 65 + 65],
                    pTB.h[:, kt * CH:(kt + 1) * CH],
                    start=(kt == 0), stop=(kt == nkt - 1)))(),
                  r=[vTa.keys(kt * 1040, (kt + 1) * 1040), (("vTa_ones", 0),),
                     pTB.keys(kt * CH, (kt + 1) * CH)],
                  w=[PAVB.keys()])
            A("scalar", lambda e: e.activation(rct.h[64:65, :], PAVA.h[64:65, :], ACT.Ln),
              r=[PAVA.keys()], w=[rct.keys()])
            A("scalar", lambda e: e.activation(rct.h[64:65, :], rct.h[64:65, :], ACT.Exp,
                                               scale=-1.0),
              r=[rct.keys()], w=[rct.keys()])
            A("tensor", lambda e: e.matmul(PBC.h[0:64, :], ones.h[64:65, 0:64], rct.h[64:65, :],
                                           start=True, stop=True),
              r=[ones.keys(), rct.keys()], w=[PBC.keys()])
            A("scalar", lambda e: e.activation(sc_c1.h[0:64, :], PBC.h[0:64, :], ACT.Copy),
              r=[PBC.keys()], w=[sc_c1.keys()])
            A("scalar", lambda e: e.activation(sc_n1.h[0:64, :], PAVA.h[0:64, :], ACT.Copy),
              r=[PAVA.keys()], w=[sc_n1.keys()])
            A("vector", (lambda p=p, q0=q0, q1=q1: lambda e: e.tensor_tensor(
                attnS[p].h[0:64, q0:q1], sc_n1.h[0:64, :], sc_c1.h[0:64, :], ALU.mult))(),
              r=[sc_n1.keys(), sc_c1.keys()], w=[attnS[p].keys(q0, q1)])
            A("scalar", lambda e: e.activation(rct.h[64:65, :], PAVB.h[64:65, :], ACT.Ln),
              r=[PAVB.keys()], w=[rct.keys()])
            A("scalar", lambda e: e.activation(rct.h[64:65, :], rct.h[64:65, :], ACT.Exp,
                                               scale=-1.0),
              r=[rct.keys()], w=[rct.keys()])
            A("tensor", lambda e: e.matmul(PBC.h[0:64, :], ones.h[64:65, 0:64], rct.h[64:65, :],
                                           start=True, stop=True),
              r=[ones.keys(), rct.keys()], w=[PBC.keys()])
            A("scalar", lambda e: e.activation(sc_c1.h[0:64, :], PBC.h[0:64, :], ACT.Copy),
              r=[PBC.keys()], w=[sc_c1.keys()])
            A("scalar", lambda e: e.activation(sc_n1.h[0:64, :], PAVB.h[0:64, :], ACT.Copy),
              r=[PAVB.keys()], w=[sc_n1.keys()])
            A("vector", (lambda q0=q0, q1=q1: lambda e: e.tensor_tensor(
                tmpB.h[0:64, q0:q1], sc_n1.h[0:64, :], sc_c1.h[0:64, :], ALU.mult))(),
              r=[sc_n1.keys(), sc_c1.keys()], w=[tmpB.keys(q0, q1)])
            if hook is not None:
                hook(qc)
        A("gpsimd", (lambda p=p: lambda e: e.dma_start(
            out=attnS[p].h[64:128, :], in_=tmpB.h[0:64, :]))(),
          r=[tmpB.keys(0, S)], w=[attnS[p].keys(0, S), (("shiftchain", 0),)], dma="dsh")

    LN2C = 0.6931471805599453

    def chain_ops(s, tc, ec, SC, xi, ci=0):
        c0, c1 = tc * CH, (tc + 1) * CH
        t, cc, n1, w_, sq, ff = SC["t"], SC["c1"], SC["n1"], SC["w"], SC["sq"], SC["ff"]
        ak = attnS[ec].keys(c0, c1)
        xk = (("xsub", xi),)
        ops = []
        def O(eng, fn, r=(), w=(), dma=None):
            ops.append((eng, fn, list(r), list(w), dma))
        O("sync", x_dma(xi, s, ec, tc), w=[xk], dma=f"dx{xi}")
        O("vector", (lambda xi=xi, ec=ec, c0=c0, c1=c1, t=t: lambda e: e.tensor_tensor(
            t.h[:], xsub[xi].h[:], attnS[ec].h[:, c0:c1], ALU.add))(),
          r=[xk, ak], w=[t.keys()])
        O("vector", (lambda t=t, cc=cc: lambda e: e.tensor_tensor(
            cc.h[:], t.h[:], P0.h[:], ALU.subtract))(),
          r=[t.keys(), P0.keys()], w=[cc.keys()])
        O("vector", (lambda cc=cc, w_=w_: lambda e: e.tensor_tensor(
            w_.h[:], cc.h[:], P1.h[:], ALU.mult))(),
          r=[cc.keys(), P1.keys()], w=[w_.keys()])
        O("scalar", (lambda ec=ec, c0=c0, c1=c1, w_=w_: lambda e: e.activation(
            attnS[ec].h[:, c0:c1], w_.h[:], ACT.Copy))(),
          r=[w_.keys()], w=[ak])
        O("vector", (lambda t=t, w_=w_: lambda e: e.tensor_scalar(
            t.h[:], w_.h[:], c448col.h[:, 0:1], m448col.h[:, 0:1], ALU.min, ALU.max))(),
          r=[w_.keys(), c448col.keys(), m448col.keys()], w=[t.keys()])
        O("gpsimd", (lambda t=t, cc=cc: lambda e: e.tensor_tensor(
            cc.h[:], t.h[:], t.h[:], ALU.mult))(), r=[t.keys()], w=[cc.keys()])
        O("scalar", (lambda cc=cc: lambda e: e.activation(
            cc.h[:], cc.h[:], ACT.Ln, scale=ri2col.h[:, 0:1], bias=epscol.h[:, 0:1]))(),
          r=[cc.keys(), ri2col.keys(), epscol.keys()], w=[cc.keys()])
        O("vector", (lambda cc=cc: lambda e: e.tensor_scalar(
            cc.h[:], cc.h[:], 0.7213475204444817, -6.0, ALU.mult, ALU.max))(),
          r=[cc.keys()], w=[cc.keys()])
        O("vector", (lambda cc=cc, ff=ff: lambda e: e.tensor_scalar(
            ff.h[:], cc.h[:], 2.0, -8.0, ALU.is_ge, ALU.mult))(),
          r=[cc.keys()], w=[ff.keys()])
        O("vector", (lambda cc=cc, ff=ff, w_=w_: lambda e: e.scalar_tensor_tensor(
            w_.h[:], cc.h[:], 6.0, ff.h[:], ALU.add, ALU.add))(),
          r=[cc.keys(), ff.keys()], w=[w_.keys()])
        for li, L in enumerate((4.0, 2.0, 1.0)):
            peng = "vector" if (li + ci) % 4 == 0 else "gpsimd"
            O("vector", (lambda L=L, w_=w_, ff=ff: lambda e: e.tensor_scalar(
                ff.h[:], w_.h[:], L, -L, ALU.is_ge, ALU.mult))(),
              r=[w_.keys()], w=[ff.keys()])
            O(peng, (lambda w_=w_, ff=ff: lambda e: e.tensor_tensor(
                w_.h[:], w_.h[:], ff.h[:], ALU.add))(),
              r=[w_.keys(), ff.keys()], w=[w_.keys()])
        O("vector", (lambda cc=cc, w_=w_, n1=n1: lambda e: e.scalar_tensor_tensor(
            n1.h[:], cc.h[:], 6.0, w_.h[:], ALU.add, ALU.subtract))(),
          r=[cc.keys(), w_.keys()], w=[n1.keys()])
        O("vector", (lambda n1=n1: lambda e: e.tensor_scalar(
            n1.h[:], n1.h[:], -LN2C, 9.0 * LN2C, ALU.mult, ALU.add))(),
          r=[n1.keys()], w=[n1.keys()])
        O("scalar", (lambda n1=n1, cc=cc: lambda e: e.activation(
            cc.h[:], n1.h[:], ACT.Exp, bias=lnricol.h[:, 0:1]))(),
          r=[n1.keys(), lnricol.keys()], w=[cc.keys()])
        O("scalar", (lambda n1=n1, sq=sq: lambda e: e.activation(
            sq.h[:], n1.h[:], ACT.Exp, scale=-1.0, bias=lniscol.h[:, 0:1]))(),
          r=[n1.keys(), lniscol.keys()], w=[sq.keys()])
        O("gpsimd", (lambda t=t, cc=cc: lambda e: e.tensor_tensor(
            t.h[:], t.h[:], cc.h[:], ALU.mult))(),
          r=[t.keys(), cc.keys()], w=[t.keys()])
        O("vector", (lambda t=t, ff=ff: lambda e: e.tensor_scalar(
            ff.h[:], t.h[:], 15.5, -32.0, ALU.is_ge, ALU.mult))(),
          r=[t.keys()], w=[ff.keys()])
        O("vector", (lambda t=t, ff=ff, w_=w_: lambda e: e.scalar_tensor_tensor(
            w_.h[:], t.h[:], 16.5, ff.h[:], ALU.add, ALU.add))(),
          r=[t.keys(), ff.keys()], w=[w_.keys()])
        for li, L in enumerate((16.0, 8.0, 4.0, 2.0, 1.0)):
            peng = "vector" if (li + 3 + ci) % 4 == 0 else "gpsimd"
            O("vector", (lambda L=L, w_=w_, ff=ff: lambda e: e.tensor_scalar(
                ff.h[:], w_.h[:], L, -L, ALU.is_ge, ALU.mult))(),
              r=[w_.keys()], w=[ff.keys()])
            O(peng, (lambda w_=w_, ff=ff: lambda e: e.tensor_tensor(
                w_.h[:], w_.h[:], ff.h[:], ALU.add))(),
              r=[w_.keys(), ff.keys()], w=[w_.keys()])
        O("vector", (lambda t=t, w_=w_: lambda e: e.scalar_tensor_tensor(
            t.h[:], t.h[:], 0.5, w_.h[:], ALU.add, ALU.subtract))(),
          r=[t.keys(), w_.keys()], w=[t.keys()])
        O("vector", (lambda ec=ec, c0=c0, c1=c1, t=t, sq=sq: lambda e: e.tensor_tensor(
            z1T[ec].h[:, c0:c1], t.h[:], sq.h[:], ALU.mult))(),
          r=[t.keys(), sq.keys()], w=[z1T[ec].keys(c0, c1)])
        return ops

    def d3_group(s, tc, oc):
        c0, c1 = tc * CH, (tc + 1) * CH
        PF = PAVA if oc % 2 == 0 else PAVB
        for ec in range(EC):
            A("tensor", (lambda ec=ec, oc=oc, c0=c0, c1=c1, PF=PF: lambda e: e.matmul(
                PF.h[:], vTa.h[:, ec * 1024 + oc * 128:ec * 1024 + (oc + 1) * 128],
                z1T[ec].h[:, c0:c1], start=(ec == 0), stop=(ec == EC - 1)))(),
              r=[vTa.keys(ec * 1024, ec * 1024 + 1024), z1T[ec].keys(c0, c1)],
              w=[PF.keys()])
        A("vector", (lambda oc=oc, PF=PF: lambda e: e.tensor_scalar(
            sD_a.h[:], PF.h[:], wscol.h[:, 0:1], fbcol.h[:, oc:oc + 1],
            ALU.mult, ALU.add))(),
          r=[PF.keys(), fbcol.keys(0, EC), wscol.keys()], w=[sD_a.keys()])
        A("vector", (lambda oc=oc, c0=c0, c1=c1: lambda e: e.tensor_tensor(
            sD_b.h[:], sD_a.h[:], attnS[oc].h[:, c0:c1], ALU.add))(),
          r=[sD_a.keys(), attnS[oc].keys(c0, c1)], w=[sD_b.keys()])
        r3buf = pTA if oc < 4 else pTB
        rc0 = (oc % 4) * S + c0
        A("scalar", (lambda r3buf=r3buf, rc0=rc0: lambda e: e.activation(
            r3buf.h[:, rc0:rc0 + CH], sD_b.h[:], ACT.Copy))(),
          r=[sD_b.keys()], w=[r3buf.keys(rc0, rc0 + CH)])
        A("scalar", lambda e: e.activation(sD_a.h[:], sD_b.h[:], ACT.Square),
          r=[sD_b.keys()], w=[sD_a.keys()])
        A("tensor", (lambda oc=oc: lambda e: e.matmul(
            P2.h[0:1, :], ones.h[0:128, 0:1], sD_b.h[:],
            start=(oc == 0), stop=(oc == EC - 1)))(),
          r=[sD_b.keys(), ones.keys()], w=[P2.keys()])
        A("tensor", (lambda oc=oc: lambda e: e.matmul(
            P3.h[0:1, :], ones.h[0:128, 0:1], sD_a.h[:],
            start=(oc == 0), stop=(oc == EC - 1)))(),
          r=[sD_a.keys(), ones.keys()], w=[P3.keys()])

    def d4_block(s, tc):
        c0, c1 = tc * CH, (tc + 1) * CH
        A("scalar", lambda e: e.activation(murow.h[:], P2.h[0:1, :], ACT.Copy, scale=1.0 / E),
          r=[P2.keys()], w=[murow.keys()])
        A("scalar", lambda e: e.activation(rsrow.h[:], P3.h[0:1, :], ACT.Copy, scale=1.0 / E),
          r=[P3.keys()], w=[rsrow.keys()])
        A("vector", lambda e: e.tensor_tensor(PM.h[0:1, :], murow.h[:], murow.h[:], ALU.mult),
          r=[murow.keys()], w=[PM.keys()])
        A("vector", lambda e: e.tensor_tensor(rsrow.h[:], rsrow.h[:], PM.h[0:1, :], ALU.subtract),
          r=[PM.keys(), rsrow.keys()], w=[rsrow.keys()])
        A("vector", lambda e: e.tensor_scalar(rsrow.h[:], rsrow.h[:], 1e-5, None, ALU.add),
          r=[rsrow.keys()], w=[rsrow.keys()])
        A("scalar", lambda e: e.activation(rsrow.h[:], rsrow.h[:], ACT.Ln),
          r=[rsrow.keys()], w=[rsrow.keys()])
        A("scalar", lambda e: e.activation(rsrow.h[:], rsrow.h[:], ACT.Exp, scale=-0.5),
          r=[rsrow.keys()], w=[rsrow.keys()])
        A("tensor", lambda e: e.matmul(PBC.h[:], ones.h[0:1, 0:128], murow.h[:],
                                       start=True, stop=True),
          r=[ones.keys(), murow.keys()], w=[PBC.keys()])
        A("tensor", lambda e: e.matmul(PM.h[:], ones.h[0:1, 0:128], rsrow.h[:],
                                       start=True, stop=True),
          r=[ones.keys(), rsrow.keys()], w=[PM.keys()])
        for oc in range(EC):
            r3buf = pTA if oc < 4 else pTB
            rc0 = (oc % 4) * S + c0
            A("vector", (lambda r3buf=r3buf, rc0=rc0: lambda e: e.tensor_tensor(
                sD_a.h[:], r3buf.h[:, rc0:rc0 + CH], PBC.h[:], ALU.subtract))(),
              r=[r3buf.keys(rc0, rc0 + CH), PBC.keys()], w=[sD_a.keys()])
            A("vector", lambda e: e.tensor_tensor(sD_b.h[:], sD_a.h[:], PM.h[:], ALU.mult),
              r=[sD_a.keys(), PM.keys()], w=[sD_b.keys()])
            A("sync", (lambda s=s, oc=oc, c0=c0, c1=c1: lambda e: e.dma_start(
                out=outT[s, oc * 128:(oc + 1) * 128, c0:c1], in_=sD_b.h[:]))(),
              r=[sD_b.keys()], w=[(("out", (s, oc, tc)),)], dma="dout")

    def phaseD(s):
        A("gpsimd", wffn_dma, w=[vTa.keys(0, EC * 1024), (("vTa_ones", 0),)], dma="dwf")
        setA = {"t": sc_t, "c1": sc_c1, "n1": sc_n1, "w": sc_w, "sq": sc_sq, "ff": sc_ff}
        setB = {"t": sB_t, "c1": sB_c1, "n1": sB_n1, "w": sB_w, "sq": sB_sq, "ff": sB_ff}
        setC = {"t": sC_t, "c1": sC_c1, "n1": sC_n1, "w": sC_w, "sq": sC_sq, "ff": sC_ff}
        SETS = (setA, setB, setC)
        for tc in range(TC):
            ln_tc(s, "r2", tc)
            groups = ((0, 1, 2), (3, 4, 5), (6, 7))
            d3q = [(2 * g + o) for g in range(3) for o in range(3) if 2 * g + o < 8]
            nd3 = 0
            for gi, grp in enumerate(groups):
                streams = [chain_ops(s, tc, ec, SETS[j], ec % 4, ci=j)
                           for j, ec in enumerate(grp)]
                for i in range(max(len(o) for o in streams)):
                    for ops in streams:
                        if i < len(ops):
                            eng, fn, r, w, dma = ops[i]
                            A(eng, fn, r=r, w=w, dma=dma)
                if tc >= 1:
                    take = 3 if gi < 2 else 2
                    for _ in range(take):
                        d3_group(s, tc - 1, nd3)
                        nd3 += 1
            if tc >= 1:
                d4_block(s, tc - 1)
        for pp in range(4):
            d3_group(s, TC - 1, 2 * pp)
            d3_group(s, TC - 1, 2 * pp + 1)
        d4_block(s, TC - 1)

    for s in range(B):
        phaseA(s)
        phaseB(s)
        A("vector", lambda e: e.memset(ka.h[64:65, :], -1.0),
          w=[ka.keys(0, 2 * S), (("ka_neg1", 0),)])
        def wqk_load(p):
            nb = p % 2
            A("gpsimd", wqk_dma(wqk[nb], p, 0), w=[(("wqkh", nb, 0),)],
              dma="dwq00" if nb == 0 else "dwq10")
            A("gpsimd", wqk_dma(wqk[nb], p, 1), w=[(("wqkh", nb, 1),)],
              dma="dwq01" if nb == 0 else "dwq11")
        wqk_load(0)
        for p in range(H // 2):
            if p + 1 < H // 2:
                wqk_load(p + 1)
            phaseB2_pair(s, p, p % 2)
            stage_dmas(s, p)
            stage_norms(s, p)
            phaseC_pair(s, p)
        phaseD(s)

    counts, per_eng = sch.finalize()

    with es:
        sems = {c: es.enter_context(nc.semaphore("c_" + c)) for c in CHANNELS}
        with nc.Block() as block:
            def emit_stream(engobj, eng):
                watermark = {}
                for op in per_eng[eng]:
                    for chan, prod in op.waits.items():
                        val = prod.tok
                        if val is None:
                            raise RuntimeError(f"unassigned token (tag={prod.tag})")
                        if watermark.get(chan, -1) < val:
                            engobj.wait_ge(sems[chan], val)
                            watermark[chan] = val
                    instr = op.fn(engobj)
                    if op.needs_inc:
                        if instr is None:
                            raise RuntimeError(f"op returned no instruction (tag={op.tag})")
                        instr.then_inc(sems[op.chan], op.inc_k)

            @block.sync
            def _(sync):
                emit_stream(sync, "sync")
                if counts["dout"]:
                    sync.wait_ge(sems["dout"], counts["dout"])

            @block.gpsimd
            def _(g):
                emit_stream(g, "gpsimd")

            @block.vector
            def _(v):
                emit_stream(v, "vector")

            @block.scalar
            def _(s_):
                emit_stream(s_, "scalar")

            @block.tensor
            def _(t):
                emit_stream(t, "tensor")

    return nc


_NC_CACHE = [None]


def _host_stage(L0_attn_x, L0_attn_gamma, L0_attn_beta, L0_attn_w_qkv,
                L0_ffn_weight_fp8, L0_ffn_bias, L0_ffn_input_scale, L0_ffn_weight_scale):
    x = np.ascontiguousarray(np.asarray(L0_attn_x, dtype=np.float32))
    wqkvT = np.ascontiguousarray(np.asarray(L0_attn_w_qkv, dtype=np.float32).T)
    wffnT = np.ascontiguousarray(np.asarray(L0_ffn_weight_fp8, dtype=np.float32).T)
    gcol = np.ascontiguousarray(np.asarray(L0_attn_gamma, np.float32).reshape(EC, 128).T)
    bcol = np.ascontiguousarray(np.asarray(L0_attn_beta, np.float32).reshape(EC, 128).T)
    fbcol = np.ascontiguousarray(np.asarray(L0_ffn_bias, np.float32).reshape(EC, 128).T)
    iscale = np.ascontiguousarray(np.asarray(L0_ffn_input_scale, np.float32).reshape(1, 1))
    wscale = np.ascontiguousarray(np.asarray(L0_ffn_weight_scale, np.float32).reshape(1, 1))
    kl = np.arange(128)[:, None]
    ql = np.arange(128)[None, :]
    trimin = np.where(kl <= ql, np.float32(1e38), np.float32(-1e30)).astype(np.float32)
    in_maps = []
    for c in range(NCORES):
        xc = np.ascontiguousarray(x[2 * c:2 * c + 2].transpose(0, 2, 1))
        in_maps.append({
            "xT": xc, "wqkvT": wqkvT, "wffnT": wffnT,
            "gcol": gcol, "bcol": bcol, "fbcol": fbcol,
            "iscale": iscale, "wscale": wscale, "trimin": trimin,
        })
    return in_maps


def kernel(L0_attn_x, L0_attn_gamma, L0_attn_beta, L0_attn_w_qkv,
           L0_ffn_weight_fp8, L0_ffn_bias, L0_ffn_input_scale, L0_ffn_weight_scale):
    in_maps = _host_stage(L0_attn_x, L0_attn_gamma, L0_attn_beta, L0_attn_w_qkv,
                          L0_ffn_weight_fp8, L0_ffn_bias, L0_ffn_input_scale,
                          L0_ffn_weight_scale)
    if _NC_CACHE[0] is None:
        _NC_CACHE[0] = build_nc()
    nc = _NC_CACHE[0]
    res = run_bass_kernel_spmd(nc, in_maps, core_ids=list(range(NCORES)))
    outs = []
    for c in range(NCORES):
        o = np.asarray(res.results[c]["outT"])
        outs.append(np.ascontiguousarray(o.transpose(0, 2, 1)))
    return np.concatenate(outs, axis=0).astype(np.float32)


# version 21 (final)
# speedup vs baseline: 1.0032x; 1.0032x over previous
import numpy as np
import ml_dtypes

import concourse.bass as bass
import concourse.mybir as mybir
from concourse.bass_utils import run_bass_kernel_spmd

F32 = mybir.dt.float32
BF16 = mybir.dt.bfloat16
I32 = mybir.dt.int32
ALU = mybir.AluOpType
ACT = mybir.ActivationFunctionType
AX = mybir.AxisListType

DEBUG_SIM = False

B, S, E, H, D = 2, 2048, 1024, 16, 64
EC = 8
TC = 4
CH = 512
NCORES = 8
MASK_AND = -262144

ENGINES = ("sync", "gpsimd", "vector", "scalar", "tensor")
DMA_FAMS = ("dx0", "dx1", "dx2", "dx3", "dwq00", "dwq01", "dwq10", "dwq11", "dwv",
            "dwf", "dsh", "dcst", "dout", "dqa0", "dqa1", "dka0", "dka1", "dmr0", "dmr1")
CHANNELS = ("TE", "SE", "VE", "GP", "SY") + DMA_FAMS
ENG2CHAN = {"tensor": "TE", "scalar": "SE", "vector": "VE", "gpsimd": "GP", "sync": "SY"}
COMPUTE_CHANS = ("TE", "SE", "VE", "GP", "SY")


class Buf:
    def __init__(self, handle, name, slot_cols):
        self.h = handle
        self.name = name
        self.slot = slot_cols

    def keys(self, c0=None, c1=None):
        if c0 is None:
            return [(self.name, -1)]
        return [(self.name, i) for i in range(c0 // self.slot, (c1 - 1) // self.slot + 1)]


class Op:
    __slots__ = ("eng", "chan", "fn", "idx", "eseq", "inc_k", "needs_inc",
                 "tok", "waits", "tag")

    def __init__(self, eng, chan, fn, idx, inc_k, tag):
        self.eng, self.chan, self.fn = eng, chan, fn
        self.idx, self.inc_k = idx, inc_k
        self.eseq = None
        self.needs_inc = False
        self.tok = None
        self.waits = {}
        self.tag = tag


class Sched:
    def __init__(self):
        self.ops = []
        self.last_w = {}
        self.readers = {}
        self.eng_seq = {e: 0 for e in ENGINES}
        self.last_drain = {e: -1 for e in ENGINES}
        self._idx = 0

    def _need_drain(self, op_eng, prod):
        return prod.eseq > self.last_drain[op_eng]

    def add(self, eng, fn, r=(), w=(), dma=None, tag=""):
        chan = dma if dma else ENG2CHAN[eng]
        rk = [k for grp in r for k in grp]
        wk = [k for grp in w for k in grp]
        self._idx += 1
        op = Op(eng, chan, fn, self._idx, 16 if dma else 1, tag)
        preds = []
        for k in rk:
            p = self.last_w.get(k)
            if p is not None:
                preds.append(p)
        for k in wk:
            p = self.last_w.get(k)
            if p is not None:
                preds.append(p)
            preds.extend(self.readers.get(k, ()))
        need_drain = False
        for p in preds:
            if p is op:
                continue
            if (dma is None) and p.chan == chan and chan in COMPUTE_CHANS:
                if DEBUG_SIM and eng != "tensor" and self._need_drain(eng, p):
                    need_drain = True
            else:
                cur = op.waits.get(p.chan)
                if cur is None or p.idx > cur.idx:
                    op.waits[p.chan] = p
                p.needs_inc = True
        if need_drain:
            self._idx += 1
            d = Op(eng, ENG2CHAN[eng], lambda e: e.drain(), self._idx - 1, 1, "drain")
            d.eseq = self.eng_seq[eng]
            self.eng_seq[eng] += 1
            self.last_drain[eng] = d.eseq
            self.ops.append(d)
        op.eseq = self.eng_seq[eng]
        self.eng_seq[eng] += 1
        self.ops.append(op)
        for k in rk:
            self.readers.setdefault(k, []).append(op)
        for k in wk:
            self.last_w[k] = op
            self.readers[k] = []
        return op

    def finalize(self):
        counts = {c: 0 for c in CHANNELS}
        for op in self.ops:
            if op.chan == "dout":
                op.needs_inc = True
            if op.needs_inc:
                counts[op.chan] += op.inc_k
                op.tok = counts[op.chan]
        per_eng = {e: [] for e in ENGINES}
        for op in self.ops:
            per_eng[op.eng].append(op)
        return counts, per_eng


def build_nc():
    nc = bass.Bass("TRN2", target_bir_lowering=False,
                   detect_race_conditions=DEBUG_SIM)

    xT = nc.dram_tensor("xT", [B, E, S], F32, kind="ExternalInput").ap()
    wqkvT = nc.dram_tensor("wqkvT", [E, 3 * E], F32, kind="ExternalInput").ap()
    wffnT_d = nc.dram_tensor("wffnT", [E, E], F32, kind="ExternalInput").ap()
    gcol_d = nc.dram_tensor("gcol", [128, EC], F32, kind="ExternalInput").ap()
    bcol_d = nc.dram_tensor("bcol", [128, EC], F32, kind="ExternalInput").ap()
    fbcol_d = nc.dram_tensor("fbcol", [128, EC], F32, kind="ExternalInput").ap()
    iscale_d = nc.dram_tensor("iscale", [1, 1], F32, kind="ExternalInput").ap()
    wscale_d = nc.dram_tensor("wscale", [1, 1], F32, kind="ExternalInput").ap()
    trimin_d = nc.dram_tensor("trimin", [128, 128], F32, kind="ExternalInput").ap()
    outT = nc.dram_tensor("outT", [B, E, S], F32, kind="ExternalOutput").ap()

    from contextlib import ExitStack
    es = ExitStack()

    def sb(name, shape, dt, slot_cols):
        return Buf(es.enter_context(nc.sbuf_tensor(name, shape, dt)), name, slot_cols)

    def ps(name, shape):
        return Buf(es.enter_context(nc.psum_tensor(name, shape, F32)), name, 1 << 30)

    z1T = [sb(f"z1T{i}", [128, S], BF16, CH) for i in range(EC)]
    attnS = [sb(f"attnS{i}", [128, S], BF16, CH) for i in range(EC)]
    vTa = sb("vTa", [128, 16 * 16 * 65], BF16, 1040)
    pTA = sb("pTA", [128, 16 * CH], BF16, CH)
    pTB = sb("pTB", [128, 16 * CH], BF16, CH)
    q2 = sb("q2", [128, S], BF16, CH)
    k2 = sb("k2", [128, S], BF16, CH)
    wqk = [sb(f"wqk{i}", [128, EC * 256], BF16, 1 << 30) for i in range(2)]
    xsub = [sb(f"xsub{i}", [128, CH], F32, 1 << 30) for i in range(4)]
    tmpB = sb("tmpB", [128, S], BF16, CH)
    qa = sb("qa", [128, 2 * S], BF16, CH)
    ka = sb("ka", [128, 2 * S], BF16, CH)
    murow = sb("murow", [1, CH], F32, 1 << 30)
    rsrow = sb("rsrow", [1, CH], F32, 1 << 30)
    kn2row = sb("kn2row", [2, S], BF16, 1 << 30)
    kpm2row = sb("kpm2row", [2, S], BF16, 1 << 30)
    mrowb = sb("mrowb", [2, CH], BF16, 1 << 30)
    ones = sb("ones", [128, 128], F32, 1 << 30)
    bd2 = sb("bd2", [128, 2], BF16, 1 << 30)
    gcol = sb("gcol_s", [128, EC], F32, 1)
    bcol = sb("bcol_s", [128, EC], F32, 1)
    fbcol = sb("fbcol_s", [128, EC], F32, 1)
    sc_is = sb("sc_is", [1, 1], F32, 1 << 30)
    sc_ws = sb("sc_ws", [1, 1], F32, 1 << 30)
    sc_ri = sb("sc_ri", [1, 1], F32, 1 << 30)
    sc_sws = sb("sc_sws", [1, 1], F32, 1 << 30)
    ri_col = sb("ri_col", [128, 1], F32, 1 << 30)
    sws_col = sb("sws_col", [128, 1], F32, 1 << 30)
    c448col = sb("c448col", [128, 1], F32, 1 << 30)
    m448col = sb("m448col", [128, 1], F32, 1 << 30)
    ri2col = sb("ri2col", [128, 1], F32, 1 << 30)
    lnricol = sb("lnricol", [128, 1], F32, 1 << 30)
    lniscol = sb("lniscol", [128, 1], F32, 1 << 30)
    wscol = sb("wscol", [128, 1], F32, 1 << 30)
    sc_c448 = sb("sc_c448", [1, 1], F32, 1 << 30)
    sc_m448 = sb("sc_m448", [1, 1], F32, 1 << 30)
    sc_ri2 = sb("sc_ri2", [1, 1], F32, 1 << 30)
    sc_lnri = sb("sc_lnri", [1, 1], F32, 1 << 30)
    sc_lnis = sb("sc_lnis", [1, 1], F32, 1 << 30)
    trimin = sb("trimin_s", [128, 128], F32, 1 << 30)
    rct = sb("rct", [128, CH], F32, 1 << 30)
    sc_t = sb("sc_t", [128, CH], F32, 1 << 30)
    sc_w = sb("sc_w", [128, CH], F32, 1 << 30)
    sc_c1 = sb("sc_c1", [128, CH], F32, 1 << 30)
    sc_n1 = sb("sc_n1", [128, CH], F32, 1 << 30)
    sc_r2 = sb("sc_r2", [128, CH], F32, 1 << 30)
    sc_sq = sb("sc_sq", [128, CH], F32, 1 << 30)
    sc_ff = sb("sc_ff", [128, CH], F32, 1 << 30)
    qsq = tmpB
    epscol = sb("epscol", [128, 1], F32, 1 << 30)

    class AliasBuf:
        __slots__ = ("h", "name", "_keys")
        def __init__(self, name, h, keys):
            self.name, self.h, self._keys = name, h, keys
        def keys(self, *a):
            return self._keys

    def alias_tile(nm, addr, under_keys):
        h = nc.virtual_sbuf_tensor(nm, [128, CH], F32)
        nc._tensor(h.name, h.shape, h.dtype, type="SB")
        mloc = nc.lookup_mloc(h)
        mloc.allocated = not nc.target_bir_lowering
        mloc.addr = addr
        return AliasBuf(nm, h, under_keys)

    _qa_addr = nc.lookup_mloc(qa.h).addr
    _ka_addr = nc.lookup_mloc(ka.h).addr
    sB_t = alias_tile("sB_t", _qa_addr + 0, qa.keys(0, 1024))
    sB_c1 = alias_tile("sB_c1", _qa_addr + 2048, qa.keys(1024, 2048))
    sB_n1 = alias_tile("sB_n1", _qa_addr + 4096, qa.keys(2048, 3072))
    sB_w = alias_tile("sB_w", _qa_addr + 6144, qa.keys(3072, 4096))
    sB_sq = alias_tile("sB_sq", _ka_addr + 0, ka.keys(0, 1024))
    sB_ff = alias_tile("sB_ff", _ka_addr + 2048, ka.keys(1024, 2048))
    sD_a = alias_tile("sD_a", _ka_addr + 4096, ka.keys(2048, 3072))
    sD_b = alias_tile("sD_b", _ka_addr + 6144, ka.keys(3072, 4096))
    _q2_addr = nc.lookup_mloc(q2.h).addr
    _k2_addr = nc.lookup_mloc(k2.h).addr
    _tmpB_addr = nc.lookup_mloc(tmpB.h).addr
    sC_t = alias_tile("sC_t", _q2_addr + 0, q2.keys(0, 1024))
    sC_c1 = alias_tile("sC_c1", _q2_addr + 2048, q2.keys(1024, 2048))
    sC_n1 = alias_tile("sC_n1", _k2_addr + 0, k2.keys(0, 1024))
    sC_w = alias_tile("sC_w", _k2_addr + 2048, k2.keys(1024, 2048))
    sC_sq = alias_tile("sC_sq", _tmpB_addr + 0, tmpB.keys(0, 1024))
    sC_ff = alias_tile("sC_ff", _tmpB_addr + 2048, tmpB.keys(1024, 2048))

    PW0 = ps("PW0", [128, 2 * CH])
    PW1 = ps("PW1", [128, 2 * CH])

    class _HalfH:
        __slots__ = ("th", "c0")
        def __init__(self, th, c0):
            self.th, self.c0 = th, c0
        def __getitem__(self, idx):
            if isinstance(idx, tuple):
                rows, cols = idx
            else:
                rows, cols = idx, slice(None)
            cs = cols.start or 0
            ce = cols.stop if cols.stop is not None else CH
            return self.th[rows, self.c0 + cs:self.c0 + ce]

    class _HalfBuf:
        __slots__ = ("h", "name")
        def __init__(self, pw, c0, name):
            self.h = _HalfH(pw.h, c0)
            self.name = name
        def keys(self, *a):
            return [(self.name, -1)]

    P0 = _HalfBuf(PW0, 0, "P0h")
    P1 = _HalfBuf(PW0, CH, "P1h")
    P2 = _HalfBuf(PW1, 0, "P2h")
    P3 = _HalfBuf(PW1, CH, "P3h")
    PAVA = ps("PAVA", [128, CH])
    PAVB = ps("PAVB", [128, CH])
    PBC = ps("PBC", [128, CH])
    PM = ps("PM", [128, CH])

    va3 = vTa.h[:].rearrange("p (t c) -> p t c", c=65)

    sch = Sched()
    A = sch.add

    A("sync", lambda e: e.dma_start(out=gcol.h[:], in_=gcol_d), w=[gcol.keys(0, EC), (("cstchain", 0),)], dma="dcst")
    A("sync", lambda e: e.dma_start(out=bcol.h[:], in_=bcol_d), w=[bcol.keys(0, EC), (("cstchain", 0),)], dma="dcst")
    A("sync", lambda e: e.dma_start(out=fbcol.h[:], in_=fbcol_d), w=[fbcol.keys(0, EC), (("cstchain", 0),)], dma="dcst")
    A("sync", lambda e: e.dma_start(out=sc_is.h[:], in_=iscale_d), w=[sc_is.keys(), (("cstchain", 0),)], dma="dcst")
    A("sync", lambda e: e.dma_start(out=sc_ws.h[:], in_=wscale_d), w=[sc_ws.keys(), (("cstchain", 0),)], dma="dcst")
    A("sync", lambda e: e.dma_start(out=trimin.h[:], in_=trimin_d), w=[trimin.keys(), (("cstchain", 0),)], dma="dcst")
    A("vector", lambda e: e.memset(ones.h[:], 1.0), w=[ones.keys()])
    A("vector", lambda e: e.memset(bd2.h[:], 0.0), w=[bd2.keys()])
    A("vector", lambda e: e.memset(bd2.h[0:64, 0:1], 1.0), w=[bd2.keys()])
    A("vector", lambda e: e.memset(bd2.h[64:128, 1:2], 1.0), w=[bd2.keys()])
    A("vector", lambda e: e.memset(epscol.h[:], 1e-38), w=[epscol.keys()])
    A("vector", lambda e: e.reciprocal(sc_ri.h[:], sc_is.h[:]), r=[sc_is.keys()], w=[sc_ri.keys()])
    A("vector", lambda e: e.tensor_tensor(sc_sws.h[:], sc_is.h[:], sc_ws.h[:], ALU.mult),
      r=[sc_is.keys(), sc_ws.keys()], w=[sc_sws.keys()])
    A("tensor", lambda e: e.matmul(PM.h[0:128, 0:1], ones.h[0:1, 0:128], sc_ri.h[0:1, 0:1],
                                   start=True, stop=True),
      r=[ones.keys(), sc_ri.keys()], w=[PM.keys()])
    A("vector", lambda e: e.tensor_copy(ri_col.h[:], PM.h[0:128, 0:1]),
      r=[PM.keys()], w=[ri_col.keys()])
    A("tensor", lambda e: e.matmul(PM.h[0:128, 0:1], ones.h[0:1, 0:128], sc_sws.h[0:1, 0:1],
                                   start=True, stop=True),
      r=[ones.keys(), sc_sws.keys()], w=[PM.keys()])
    A("vector", lambda e: e.tensor_copy(sws_col.h[:], PM.h[0:128, 0:1]),
      r=[PM.keys()], w=[sws_col.keys()])
    A("vector", lambda e: e.tensor_scalar(sc_c448.h[:], sc_is.h[:], 448.0, None, ALU.mult),
      r=[sc_is.keys()], w=[sc_c448.keys()])
    A("vector", lambda e: e.tensor_scalar(sc_m448.h[:], sc_is.h[:], -448.0, None, ALU.mult),
      r=[sc_is.keys()], w=[sc_m448.keys()])
    A("vector", lambda e: e.tensor_tensor(sc_ri2.h[:], sc_ri.h[:], sc_ri.h[:], ALU.mult),
      r=[sc_ri.keys()], w=[sc_ri2.keys()])
    A("scalar", lambda e: e.activation(sc_lnri.h[:], sc_ri.h[:], ACT.Ln),
      r=[sc_ri.keys()], w=[sc_lnri.keys()])
    A("scalar", lambda e: e.activation(sc_lnis.h[:], sc_is.h[:], ACT.Ln),
      r=[sc_is.keys()], w=[sc_lnis.keys()])
    for _src, _dst in ((sc_c448, c448col), (sc_m448, m448col), (sc_ri2, ri2col),
                       (sc_lnri, lnricol), (sc_lnis, lniscol), (sc_ws, wscol)):
        A("tensor", (lambda _src=_src: lambda e: e.matmul(
            PM.h[0:128, 0:1], ones.h[0:1, 0:128], _src.h[0:1, 0:1],
            start=True, stop=True))(),
          r=[ones.keys(), _src.keys()], w=[PM.keys()])
        A("vector", (lambda _dst=_dst: lambda e: e.tensor_copy(
            _dst.h[:], PM.h[0:128, 0:1]))(),
          r=[PM.keys()], w=[_dst.keys()])

    def x_dma(i, s, ec, tc):
        return lambda e: e.dma_start(
            out=xsub[i].h[:], in_=xT[s, ec * 128:(ec + 1) * 128, tc * CH:(tc + 1) * CH])

    def wqk_dma(buf, p, j):
        c0 = 128 * p if j == 0 else E + 128 * p
        def fn(e):
            src = wqkvT[:, c0:c0 + 128].rearrange("(c p) n -> p c n", p=128)
            dst = buf.h[:].rearrange("p (c n) -> p c n", n=256)[:, :, j * 128:(j + 1) * 128]
            return e.dma_start(out=dst, in_=src)
        return fn

    def wv_dma(e):
        src = wqkvT[:, 2 * E:3 * E].rearrange("(c p) n -> p c n", p=128)
        dst = pTA.h[:].rearrange("p (c n) -> p c n", n=1024)
        return e.dma_start(out=dst, in_=src)

    def wffn_dma(e):
        src = wffnT_d.rearrange("(c p) n -> p c n", p=128)
        dst = vTa.h[:, 0:EC * 1024].rearrange("p (c n) -> p c n", n=1024)
        return e.dma_start(out=dst, in_=src)

    def ln_tc(s, src_kind, tc):
        xi = (tc * EC) % 4
        for ec in range(EC):
            i = xi
            xk = (("xsub", i),)
            A("sync", x_dma(i, s, ec, tc), w=[xk], dma=f"dx{i}")
            if src_kind == "x":
                srcb, srck = xsub[i].h, [xk]
            else:
                A("vector", (lambda i=i, ec=ec, tc=tc: lambda e: e.tensor_tensor(
                    sc_r2.h[:], xsub[i].h[:], attnS[ec].h[:, tc * CH:(tc + 1) * CH],
                    ALU.add))(),
                  r=[xk, attnS[ec].keys(tc * CH, (tc + 1) * CH)], w=[sc_r2.keys()])
                srcb, srck = sc_r2.h, [sc_r2.keys()]
            A("scalar", (lambda srcb=srcb: lambda e: e.activation(
                rct.h[:], srcb[:], ACT.Square))(), r=srck, w=[rct.keys()])
            A("tensor", (lambda srcb=srcb, ec=ec: lambda e: e.matmul(
                P2.h[0:1, :], ones.h[0:128, 0:1], srcb[:],
                start=(ec == 0), stop=(ec == EC - 1)))(),
              r=srck + [ones.keys()], w=[P2.keys()])
            A("tensor", (lambda ec=ec: lambda e: e.matmul(
                P3.h[0:1, :], ones.h[0:128, 0:1], rct.h[:],
                start=(ec == 0), stop=(ec == EC - 1)))(),
              r=[rct.keys(), ones.keys()], w=[P3.keys()])
            xi = (xi + 1) % 4
        A("scalar", lambda e: e.activation(murow.h[:], P2.h[0:1, :], ACT.Copy, scale=1.0 / E),
          r=[P2.keys()], w=[murow.keys()])
        A("scalar", lambda e: e.activation(rsrow.h[:], P3.h[0:1, :], ACT.Copy, scale=1.0 / E),
          r=[P3.keys()], w=[rsrow.keys()])
        A("vector", lambda e: e.tensor_tensor(PM.h[0:1, :], murow.h[:], murow.h[:], ALU.mult),
          r=[murow.keys()], w=[PM.keys()])
        A("vector", lambda e: e.tensor_tensor(rsrow.h[:], rsrow.h[:], PM.h[0:1, :], ALU.subtract),
          r=[PM.keys(), rsrow.keys()], w=[rsrow.keys()])
        A("vector", lambda e: e.tensor_scalar(rsrow.h[:], rsrow.h[:], 1e-5, None, ALU.add),
          r=[rsrow.keys()], w=[rsrow.keys()])
        A("scalar", lambda e: e.activation(rsrow.h[:], rsrow.h[:], ACT.Ln),
          r=[rsrow.keys()], w=[rsrow.keys()])
        A("scalar", lambda e: e.activation(rsrow.h[:], rsrow.h[:], ACT.Exp, scale=-0.5),
          r=[rsrow.keys()], w=[rsrow.keys()])
        A("tensor", lambda e: e.matmul(P0.h[:], ones.h[0:1, 0:128], murow.h[:],
                                       start=True, stop=True),
          r=[ones.keys(), murow.keys()], w=[P0.keys()])
        A("tensor", lambda e: e.matmul(P1.h[:], ones.h[0:1, 0:128], rsrow.h[:],
                                       start=True, stop=True),
          r=[ones.keys(), rsrow.keys()], w=[P1.keys()])

    def phaseA(s):
        for tc in range(TC):
            ln_tc(s, "x", tc)
            c0, c1 = tc * CH, (tc + 1) * CH
            xi = (tc * EC) % 4
            for ec in range(EC):
                i = xi
                xk = (("xsub", i),)
                A("sync", x_dma(i, s, ec, tc), w=[xk], dma=f"dx{i}")
                A("vector", (lambda i=i: lambda e: e.tensor_tensor(
                    sc_t.h[:], xsub[i].h[:], P0.h[:], ALU.subtract))(),
                  r=[xk, P0.keys()], w=[sc_t.keys()])
                A("vector", lambda e: e.tensor_tensor(sc_w.h[:], sc_t.h[:], P1.h[:], ALU.mult),
                  r=[sc_t.keys(), P1.keys()], w=[sc_w.keys()])
                A("vector", (lambda ec=ec, c0=c0, c1=c1: lambda e: e.tensor_scalar(
                    z1T[ec].h[:, c0:c1], sc_w.h[:], gcol.h[:, ec:ec + 1],
                    bcol.h[:, ec:ec + 1], ALU.mult, ALU.add))(),
                  r=[sc_w.keys(), gcol.keys(0, EC), bcol.keys(0, EC)],
                  w=[z1T[ec].keys(c0, c1)])
                xi = (xi + 1) % 4

    def phaseB(s):
        A("gpsimd", wv_dma, w=[pTA.keys(0, 16 * CH)], dma="dwv")
        A("vector", lambda e: e.memset(va3[:, :, 64:65], 1.0),
          w=[vTa.keys(0, 16 * 16 * 65), (("vTa_ones", 0),)])
        for tt in range(16):
            for half in range(2):
                PT = P2 if half == 0 else P3
                for ec in range(EC):
                    A("tensor", (lambda tt=tt, half=half, ec=ec, PT=PT: lambda e: e.matmul(
                        PT.h[:], z1T[ec].h[:, tt * 128:(tt + 1) * 128],
                        pTA.h[:, ec * 1024 + half * CH:ec * 1024 + (half + 1) * CH],
                        start=(ec == 0), stop=(ec == EC - 1)))(),
                      r=[z1T[ec].keys(tt * 128, (tt + 1) * 128),
                         pTA.keys(ec * 1024 + half * CH, ec * 1024 + (half + 1) * CH)],
                      w=[PT.keys()])
                h0 = half * 8
                A("vector", (lambda tt=tt, h0=h0, PT=PT: lambda e: e.tensor_copy(
                    va3[:, tt * 16 + h0:tt * 16 + h0 + 8, 0:64],
                    PT.h[:].rearrange("p (h d) -> p h d", d=64)))(),
                  r=[PT.keys(), (("vTa_ones", 0),)],
                  w=[vTa.keys(tt * 1040, (tt + 1) * 1040)])

    def phaseB2_pair(s, p, wbuf):
        for tc in range(TC):
            phaseB2_tc(s, p, wbuf, tc)

    def phaseB2_tc(s, p, wbuf, tc):
        if True:
            c0, c1 = tc * CH, (tc + 1) * CH
            PQ = P0 if tc % 2 == 0 else P1
            PK = P2 if tc % 2 == 0 else P3
            for ec in range(EC):
                A("tensor", (lambda ec=ec, c0=c0, c1=c1, PQ=PQ, wbuf=wbuf: lambda e: e.matmul(
                    PQ.h[:], wqk[wbuf].h[:, ec * 256:ec * 256 + 128], z1T[ec].h[:, c0:c1],
                    start=(ec == 0), stop=(ec == EC - 1)))(),
                  r=[(("wqkh", wbuf, 0), ("wqkh", wbuf, 1)), z1T[ec].keys(c0, c1)], w=[PQ.keys()])
            for ec in range(EC):
                A("tensor", (lambda ec=ec, c0=c0, c1=c1, PK=PK, wbuf=wbuf: lambda e: e.matmul(
                    PK.h[:], wqk[wbuf].h[:, ec * 256 + 128:ec * 256 + 256],
                    z1T[ec].h[:, c0:c1],
                    start=(ec == 0), stop=(ec == EC - 1)))(),
                  r=[(("wqkh", wbuf, 0), ("wqkh", wbuf, 1)), z1T[ec].keys(c0, c1)], w=[PK.keys()])
            A("scalar", (lambda c0=c0, c1=c1, PQ=PQ: lambda e: e.activation(
                q2.h[:, c0:c1], PQ.h[:], ACT.Copy, scale=0.125))(),
              r=[PQ.keys()], w=[q2.keys(c0, c1)])
            A("vector", (lambda c0=c0, c1=c1, PK=PK: lambda e: e.tensor_copy(
                k2.h[:, c0:c1], PK.h[:]))(),
              r=[PK.keys()], w=[k2.keys(c0, c1)])

    def stage_dmas(s, p):
        A("sync", lambda e: e.dma_start(out=qa.h[0:64, 0:S], in_=q2.h[0:64, :]),
          r=[q2.keys(0, S)], w=[qa.keys(0, S)], dma="dqa0")
        A("sync", lambda e: e.dma_start(out=qa.h[0:64, S:2 * S], in_=q2.h[64:128, :]),
          r=[q2.keys(0, S)], w=[qa.keys(S, 2 * S)], dma="dqa1")
        A("sync", lambda e: e.dma_start(out=ka.h[0:64, 0:S], in_=k2.h[0:64, :]),
          r=[k2.keys(0, S)], w=[ka.keys(0, S)], dma="dka0")
        A("sync", lambda e: e.dma_start(out=ka.h[0:64, S:2 * S], in_=k2.h[64:128, :]),
          r=[k2.keys(0, S)], w=[ka.keys(S, 2 * S)], dma="dka1")

    def stage_norms(s, p):
        A("gpsimd", lambda e: e.tensor_tensor(qsq.h[:], k2.h[:], k2.h[:], ALU.mult),
          r=[k2.keys(0, S)], w=[qsq.keys(0, S)])
        for tc in range(TC):
            c0, c1 = tc * CH, (tc + 1) * CH
            A("tensor", (lambda c0=c0, c1=c1: lambda e: e.matmul(
                PM.h[0:2, :], bd2.h[:], qsq.h[:, c0:c1], start=True, stop=True))(),
              r=[bd2.keys(), qsq.keys(c0, c1)], w=[PM.keys()])
            A("vector", (lambda c0=c0: lambda e: e.tensor_copy(
                kn2row.h[0:2, c0:c0 + CH], PM.h[0:2, :]))(),
              r=[PM.keys()], w=[(("kn2", c0),)])
        A("vector", lambda e: e.tensor_tensor_scan(
            kpm2row.h[0:2, :], kn2row.h[0:2, :], kn2row.h[0:2, :], 0.0, ALU.max, ALU.max),
          r=[tuple(("kn2", c) for c in range(0, S, CH))], w=[(("kpm", 0),)])
        A("vector", lambda e: e.tensor_scalar(kpm2row.h[0:2, :], kpm2row.h[0:2, :],
                                              1.004, None, ALU.mult),
          r=[(("kpm", 0),)], w=[(("kpm", 0),)])
        A("gpsimd", lambda e: e.tensor_tensor(qsq.h[:], q2.h[:], q2.h[:], ALU.mult),
          r=[q2.keys(0, S)], w=[qsq.keys(0, S)])
        for tc in range(TC):
            c0, c1 = tc * CH, (tc + 1) * CH
            A("tensor", (lambda c0=c0, c1=c1: lambda e: e.matmul(
                PM.h[0:2, :], bd2.h[:], qsq.h[:, c0:c1], start=True, stop=True))(),
              r=[bd2.keys(), qsq.keys(c0, c1)], w=[PM.keys()])
            A("vector", (lambda c0=c0: lambda e: e.tensor_tensor(
                rct.h[0:2, :], PM.h[0:2, :], kpm2row.h[0:2, c0:c0 + CH], ALU.mult))(),
              r=[PM.keys(), (("kpm", 0),)], w=[rct.keys()])
            A("vector", lambda e: e.tensor_scalar(rct.h[0:2, :], rct.h[0:2, :],
                                                  1e-35, None, ALU.add),
              r=[rct.keys()], w=[rct.keys()])
            A("scalar", lambda e: e.activation(rct.h[0:2, :], rct.h[0:2, :], ACT.Ln),
              r=[rct.keys()], w=[rct.keys()])
            A("scalar", lambda e: e.activation(mrowb.h[:], rct.h[0:2, :], ACT.Exp, scale=0.5),
              r=[rct.keys()], w=[mrowb.keys()])
            A("sync", (lambda c0=c0: lambda e: e.dma_start(
                out=qa.h[64:65, c0:c0 + CH], in_=mrowb.h[0:1, :]))(),
              r=[mrowb.keys()], w=[(("qamr", 0, c0 // CH),)], dma="dmr0")
            A("sync", (lambda c0=c0: lambda e: e.dma_start(
                out=qa.h[64:65, S + c0:S + c0 + CH], in_=mrowb.h[1:2, :]))(),
              r=[mrowb.keys()], w=[(("qamr", 1, c0 // CH),)], dma="dmr1")

    def phaseC_pair(s, p, hook=None):
        hA, hB = 2 * p, 2 * p + 1
        for qc in range(TC):
            q0, q1 = qc * CH, (qc + 1) * CH
            nkt = 4 * qc + 4
            def sc_mm(head, kt, PH):
                off = 0 if head == 0 else S
                mr = (("qamr", head, qc),)
                A("tensor", (lambda kt=kt, q0=q0, q1=q1, off=off, PH=PH, mr=mr: lambda e: e.matmul(
                    PH.h[:], ka.h[0:65, off + kt * 128:off + (kt + 1) * 128],
                    qa.h[0:65, off + q0:off + q1], start=True, stop=True))(),
                  r=[ka.keys(off + kt * 128, off + (kt + 1) * 128), (("ka_neg1", 0),),
                     qa.keys(off + q0, off + q1), mr], w=[PH.keys()])

            nplain = 4 * qc
            for kt in range(nplain):
                PA = P0 if kt % 2 == 0 else P1
                PB = P2 if kt % 2 == 0 else P3
                sc_mm(0, kt, PA)
                sc_mm(1, kt, PB)
                A("scalar", (lambda kt=kt, PA=PA: lambda e: e.activation(
                    pTA.h[:, kt * CH:(kt + 1) * CH], PA.h[:], ACT.Exp))(),
                  r=[PA.keys()], w=[pTA.keys(kt * CH, (kt + 1) * CH)])
                A("scalar", (lambda kt=kt, PB=PB: lambda e: e.activation(
                    pTB.h[:, kt * CH:(kt + 1) * CH], PB.h[:], ACT.Exp))(),
                  r=[PB.keys()], w=[pTB.keys(kt * CH, (kt + 1) * CH)])
            for kt in range(nplain, nkt):
                PA = P0 if kt % 2 == 0 else P1
                PB = P2 if kt % 2 == 0 else P3
                jo = 128 * (kt - 4 * qc)
                sc_mm(0, kt, PA)
                sc_mm(1, kt, PB)
                A("vector", (lambda jo=jo, PA=PA: lambda e: e.tensor_tensor(
                    PA.h[:, jo:jo + 128], PA.h[:, jo:jo + 128], trimin.h[:], ALU.min))(),
                  r=[PA.keys(), trimin.keys()], w=[PA.keys()])
                A("vector", (lambda jo=jo, PB=PB: lambda e: e.tensor_tensor(
                    PB.h[:, jo:jo + 128], PB.h[:, jo:jo + 128], trimin.h[:], ALU.min))(),
                  r=[PB.keys(), trimin.keys()], w=[PB.keys()])
                A("scalar", (lambda kt=kt, jo=jo, PA=PA: lambda e: e.activation(
                    pTA.h[:, kt * CH + jo:(kt + 1) * CH], PA.h[:, jo:], ACT.Exp))(),
                  r=[PA.keys()], w=[pTA.keys(kt * CH, (kt + 1) * CH)])
                A("scalar", (lambda kt=kt, jo=jo, PB=PB: lambda e: e.activation(
                    pTB.h[:, kt * CH + jo:(kt + 1) * CH], PB.h[:, jo:], ACT.Exp))(),
                  r=[PB.keys()], w=[pTB.keys(kt * CH, (kt + 1) * CH)])
                if jo > 0:
                    A("gpsimd", (lambda kt=kt, jo=jo: lambda e: e.memset(
                        pTA.h[:, kt * CH:kt * CH + jo], 0.0))(),
                      w=[pTA.keys(kt * CH, (kt + 1) * CH)])
                    A("gpsimd", (lambda kt=kt, jo=jo: lambda e: e.memset(
                        pTB.h[:, kt * CH:kt * CH + jo], 0.0))(),
                      w=[pTB.keys(kt * CH, (kt + 1) * CH)])
            for kt in range(nkt):
                A("tensor", (lambda kt=kt, hA=hA, nkt=nkt: lambda e: e.matmul(
                    PAVA.h[0:65, :], vTa.h[:, (kt * 16 + hA) * 65:(kt * 16 + hA) * 65 + 65],
                    pTA.h[:, kt * CH:(kt + 1) * CH],
                    start=(kt == 0), stop=(kt == nkt - 1)))(),
                  r=[vTa.keys(kt * 1040, (kt + 1) * 1040), (("vTa_ones", 0),),
                     pTA.keys(kt * CH, (kt + 1) * CH)],
                  w=[PAVA.keys()])
            for kt in range(nkt):
                A("tensor", (lambda kt=kt, hB=hB, nkt=nkt: lambda e: e.matmul(
                    PAVB.h[0:65, :], vTa.h[:, (kt * 16 + hB) * 65:(kt * 16 + hB) * 65 + 65],
                    pTB.h[:, kt * CH:(kt + 1) * CH],
                    start=(kt == 0), stop=(kt == nkt - 1)))(),
                  r=[vTa.keys(kt * 1040, (kt + 1) * 1040), (("vTa_ones", 0),),
                     pTB.keys(kt * CH, (kt + 1) * CH)],
                  w=[PAVB.keys()])
            A("scalar", lambda e: e.activation(rct.h[64:65, :], PAVA.h[64:65, :], ACT.Ln),
              r=[PAVA.keys()], w=[rct.keys()])
            A("scalar", lambda e: e.activation(rct.h[64:65, :], rct.h[64:65, :], ACT.Exp,
                                               scale=-1.0),
              r=[rct.keys()], w=[rct.keys()])
            A("tensor", lambda e: e.matmul(PBC.h[0:64, :], ones.h[64:65, 0:64], rct.h[64:65, :],
                                           start=True, stop=True),
              r=[ones.keys(), rct.keys()], w=[PBC.keys()])
            A("scalar", lambda e: e.activation(sc_c1.h[0:64, :], PBC.h[0:64, :], ACT.Copy),
              r=[PBC.keys()], w=[sc_c1.keys()])
            A("scalar", lambda e: e.activation(sc_n1.h[0:64, :], PAVA.h[0:64, :], ACT.Copy),
              r=[PAVA.keys()], w=[sc_n1.keys()])
            A("vector", (lambda p=p, q0=q0, q1=q1: lambda e: e.tensor_tensor(
                attnS[p].h[0:64, q0:q1], sc_n1.h[0:64, :], sc_c1.h[0:64, :], ALU.mult))(),
              r=[sc_n1.keys(), sc_c1.keys()], w=[attnS[p].keys(q0, q1)])
            A("scalar", lambda e: e.activation(rct.h[64:65, :], PAVB.h[64:65, :], ACT.Ln),
              r=[PAVB.keys()], w=[rct.keys()])
            A("scalar", lambda e: e.activation(rct.h[64:65, :], rct.h[64:65, :], ACT.Exp,
                                               scale=-1.0),
              r=[rct.keys()], w=[rct.keys()])
            A("tensor", lambda e: e.matmul(PBC.h[0:64, :], ones.h[64:65, 0:64], rct.h[64:65, :],
                                           start=True, stop=True),
              r=[ones.keys(), rct.keys()], w=[PBC.keys()])
            A("scalar", lambda e: e.activation(sc_c1.h[0:64, :], PBC.h[0:64, :], ACT.Copy),
              r=[PBC.keys()], w=[sc_c1.keys()])
            A("scalar", lambda e: e.activation(sc_n1.h[0:64, :], PAVB.h[0:64, :], ACT.Copy),
              r=[PAVB.keys()], w=[sc_n1.keys()])
            A("vector", (lambda q0=q0, q1=q1: lambda e: e.tensor_tensor(
                tmpB.h[0:64, q0:q1], sc_n1.h[0:64, :], sc_c1.h[0:64, :], ALU.mult))(),
              r=[sc_n1.keys(), sc_c1.keys()], w=[tmpB.keys(q0, q1)])
            if hook is not None:
                hook(qc)
        A("gpsimd", (lambda p=p: lambda e: e.dma_start(
            out=attnS[p].h[64:128, :], in_=tmpB.h[0:64, :]))(),
          r=[tmpB.keys(0, S)], w=[attnS[p].keys(0, S), (("shiftchain", 0),)], dma="dsh")

    LN2C = 0.6931471805599453

    def chain_ops(s, tc, ec, SC, xi, ci=0):
        c0, c1 = tc * CH, (tc + 1) * CH
        t, cc, n1, w_, sq, ff = SC["t"], SC["c1"], SC["n1"], SC["w"], SC["sq"], SC["ff"]
        ak = attnS[ec].keys(c0, c1)
        xk = (("xsub", xi),)
        ops = []
        def O(eng, fn, r=(), w=(), dma=None):
            ops.append((eng, fn, list(r), list(w), dma))
        O("sync", x_dma(xi, s, ec, tc), w=[xk], dma=f"dx{xi}")
        O("vector", (lambda xi=xi, ec=ec, c0=c0, c1=c1, t=t: lambda e: e.tensor_tensor(
            t.h[:], xsub[xi].h[:], attnS[ec].h[:, c0:c1], ALU.add))(),
          r=[xk, ak], w=[t.keys()])
        O("vector", (lambda t=t, cc=cc: lambda e: e.tensor_tensor(
            cc.h[:], t.h[:], P0.h[:], ALU.subtract))(),
          r=[t.keys(), P0.keys()], w=[cc.keys()])
        O("vector", (lambda cc=cc, w_=w_: lambda e: e.tensor_tensor(
            w_.h[:], cc.h[:], P1.h[:], ALU.mult))(),
          r=[cc.keys(), P1.keys()], w=[w_.keys()])
        O("scalar", (lambda ec=ec, c0=c0, c1=c1, w_=w_: lambda e: e.activation(
            attnS[ec].h[:, c0:c1], w_.h[:], ACT.Copy))(),
          r=[w_.keys()], w=[ak])
        O("vector", (lambda t=t, w_=w_: lambda e: e.tensor_scalar(
            t.h[:], w_.h[:], c448col.h[:, 0:1], m448col.h[:, 0:1], ALU.min, ALU.max))(),
          r=[w_.keys(), c448col.keys(), m448col.keys()], w=[t.keys()])
        O("gpsimd", (lambda t=t, cc=cc: lambda e: e.tensor_tensor(
            cc.h[:], t.h[:], t.h[:], ALU.mult))(), r=[t.keys()], w=[cc.keys()])
        O("scalar", (lambda cc=cc: lambda e: e.activation(
            cc.h[:], cc.h[:], ACT.Ln, scale=ri2col.h[:, 0:1], bias=epscol.h[:, 0:1]))(),
          r=[cc.keys(), ri2col.keys(), epscol.keys()], w=[cc.keys()])
        O("vector", (lambda cc=cc: lambda e: e.tensor_scalar(
            cc.h[:], cc.h[:], 0.7213475204444817, -6.0, ALU.mult, ALU.max))(),
          r=[cc.keys()], w=[cc.keys()])
        O("vector", (lambda cc=cc, ff=ff: lambda e: e.tensor_scalar(
            ff.h[:], cc.h[:], 2.0, -8.0, ALU.is_ge, ALU.mult))(),
          r=[cc.keys()], w=[ff.keys()])
        O("vector", (lambda cc=cc, ff=ff, w_=w_: lambda e: e.scalar_tensor_tensor(
            w_.h[:], cc.h[:], 6.0, ff.h[:], ALU.add, ALU.add))(),
          r=[cc.keys(), ff.keys()], w=[w_.keys()])
        for li, L in enumerate((4.0, 2.0, 1.0)):
            peng = "vector" if (li + ci) % 4 == 0 else "gpsimd"
            O("vector", (lambda L=L, w_=w_, ff=ff: lambda e: e.tensor_scalar(
                ff.h[:], w_.h[:], L, -L, ALU.is_ge, ALU.mult))(),
              r=[w_.keys()], w=[ff.keys()])
            O(peng, (lambda w_=w_, ff=ff: lambda e: e.tensor_tensor(
                w_.h[:], w_.h[:], ff.h[:], ALU.add))(),
              r=[w_.keys(), ff.keys()], w=[w_.keys()])
        O("vector", (lambda cc=cc, w_=w_, n1=n1: lambda e: e.scalar_tensor_tensor(
            n1.h[:], cc.h[:], 6.0, w_.h[:], ALU.add, ALU.subtract))(),
          r=[cc.keys(), w_.keys()], w=[n1.keys()])
        O("vector", (lambda n1=n1: lambda e: e.tensor_scalar(
            n1.h[:], n1.h[:], -LN2C, 9.0 * LN2C, ALU.mult, ALU.add))(),
          r=[n1.keys()], w=[n1.keys()])
        O("scalar", (lambda n1=n1, cc=cc: lambda e: e.activation(
            cc.h[:], n1.h[:], ACT.Exp, bias=lnricol.h[:, 0:1]))(),
          r=[n1.keys(), lnricol.keys()], w=[cc.keys()])
        O("scalar", (lambda n1=n1, sq=sq: lambda e: e.activation(
            sq.h[:], n1.h[:], ACT.Exp, scale=-1.0, bias=lniscol.h[:, 0:1]))(),
          r=[n1.keys(), lniscol.keys()], w=[sq.keys()])
        O("gpsimd", (lambda t=t, cc=cc: lambda e: e.tensor_tensor(
            t.h[:], t.h[:], cc.h[:], ALU.mult))(),
          r=[t.keys(), cc.keys()], w=[t.keys()])
        O("vector", (lambda t=t, ff=ff: lambda e: e.tensor_scalar(
            ff.h[:], t.h[:], 15.5, -32.0, ALU.is_ge, ALU.mult))(),
          r=[t.keys()], w=[ff.keys()])
        O("vector", (lambda t=t, ff=ff, w_=w_: lambda e: e.scalar_tensor_tensor(
            w_.h[:], t.h[:], 16.5, ff.h[:], ALU.add, ALU.add))(),
          r=[t.keys(), ff.keys()], w=[w_.keys()])
        for li, L in enumerate((16.0, 8.0, 4.0, 2.0, 1.0)):
            peng = "vector" if (li + 3 + ci) % 4 == 0 else "gpsimd"
            O("vector", (lambda L=L, w_=w_, ff=ff: lambda e: e.tensor_scalar(
                ff.h[:], w_.h[:], L, -L, ALU.is_ge, ALU.mult))(),
              r=[w_.keys()], w=[ff.keys()])
            O(peng, (lambda w_=w_, ff=ff: lambda e: e.tensor_tensor(
                w_.h[:], w_.h[:], ff.h[:], ALU.add))(),
              r=[w_.keys(), ff.keys()], w=[w_.keys()])
        O("vector", (lambda t=t, w_=w_: lambda e: e.scalar_tensor_tensor(
            t.h[:], t.h[:], 0.5, w_.h[:], ALU.add, ALU.subtract))(),
          r=[t.keys(), w_.keys()], w=[t.keys()])
        O("vector", (lambda ec=ec, c0=c0, c1=c1, t=t, sq=sq: lambda e: e.tensor_tensor(
            z1T[ec].h[:, c0:c1], t.h[:], sq.h[:], ALU.mult))(),
          r=[t.keys(), sq.keys()], w=[z1T[ec].keys(c0, c1)])
        return ops

    def d3_group(s, tc, oc):
        c0, c1 = tc * CH, (tc + 1) * CH
        PF = PAVA if oc % 2 == 0 else PAVB
        for ec in range(EC):
            A("tensor", (lambda ec=ec, oc=oc, c0=c0, c1=c1, PF=PF: lambda e: e.matmul(
                PF.h[:], vTa.h[:, ec * 1024 + oc * 128:ec * 1024 + (oc + 1) * 128],
                z1T[ec].h[:, c0:c1], start=(ec == 0), stop=(ec == EC - 1)))(),
              r=[vTa.keys(ec * 1024, ec * 1024 + 1024), z1T[ec].keys(c0, c1)],
              w=[PF.keys()])
        A("scalar", (lambda oc=oc, PF=PF: lambda e: e.activation(
            sD_a.h[:], PF.h[:], ACT.Copy, scale=wscol.h[:, 0:1]))(),
          r=[PF.keys(), wscol.keys()], w=[sD_a.keys()])
        A("vector", (lambda oc=oc, c0=c0, c1=c1: lambda e: e.scalar_tensor_tensor(
            sD_b.h[:], sD_a.h[:], fbcol.h[:, oc:oc + 1], attnS[oc].h[:, c0:c1],
            ALU.add, ALU.add))(),
          r=[sD_a.keys(), fbcol.keys(0, EC), attnS[oc].keys(c0, c1)], w=[sD_b.keys()])
        r3buf = pTA if oc < 4 else pTB
        rc0 = (oc % 4) * S + c0
        A("scalar", (lambda r3buf=r3buf, rc0=rc0: lambda e: e.activation(
            r3buf.h[:, rc0:rc0 + CH], sD_b.h[:], ACT.Copy))(),
          r=[sD_b.keys()], w=[r3buf.keys(rc0, rc0 + CH)])
        A("scalar", lambda e: e.activation(sD_a.h[:], sD_b.h[:], ACT.Square),
          r=[sD_b.keys()], w=[sD_a.keys()])
        A("tensor", (lambda oc=oc: lambda e: e.matmul(
            P2.h[0:1, :], ones.h[0:128, 0:1], sD_b.h[:],
            start=(oc == 0), stop=(oc == EC - 1)))(),
          r=[sD_b.keys(), ones.keys()], w=[P2.keys()])
        A("tensor", (lambda oc=oc: lambda e: e.matmul(
            P3.h[0:1, :], ones.h[0:128, 0:1], sD_a.h[:],
            start=(oc == 0), stop=(oc == EC - 1)))(),
          r=[sD_a.keys(), ones.keys()], w=[P3.keys()])

    def d4_block(s, tc):
        c0, c1 = tc * CH, (tc + 1) * CH
        A("scalar", lambda e: e.activation(murow.h[:], P2.h[0:1, :], ACT.Copy, scale=1.0 / E),
          r=[P2.keys()], w=[murow.keys()])
        A("scalar", lambda e: e.activation(rsrow.h[:], P3.h[0:1, :], ACT.Copy, scale=1.0 / E),
          r=[P3.keys()], w=[rsrow.keys()])
        A("vector", lambda e: e.tensor_tensor(PM.h[0:1, :], murow.h[:], murow.h[:], ALU.mult),
          r=[murow.keys()], w=[PM.keys()])
        A("vector", lambda e: e.tensor_tensor(rsrow.h[:], rsrow.h[:], PM.h[0:1, :], ALU.subtract),
          r=[PM.keys(), rsrow.keys()], w=[rsrow.keys()])
        A("vector", lambda e: e.tensor_scalar(rsrow.h[:], rsrow.h[:], 1e-5, None, ALU.add),
          r=[rsrow.keys()], w=[rsrow.keys()])
        A("scalar", lambda e: e.activation(rsrow.h[:], rsrow.h[:], ACT.Ln),
          r=[rsrow.keys()], w=[rsrow.keys()])
        A("scalar", lambda e: e.activation(rsrow.h[:], rsrow.h[:], ACT.Exp, scale=-0.5),
          r=[rsrow.keys()], w=[rsrow.keys()])
        A("tensor", lambda e: e.matmul(PBC.h[:], ones.h[0:1, 0:128], murow.h[:],
                                       start=True, stop=True),
          r=[ones.keys(), murow.keys()], w=[PBC.keys()])
        A("tensor", lambda e: e.matmul(PM.h[:], ones.h[0:1, 0:128], rsrow.h[:],
                                       start=True, stop=True),
          r=[ones.keys(), rsrow.keys()], w=[PM.keys()])
        for oc in range(EC):
            r3buf = pTA if oc < 4 else pTB
            rc0 = (oc % 4) * S + c0
            A("vector", (lambda r3buf=r3buf, rc0=rc0: lambda e: e.tensor_tensor(
                sD_a.h[:], r3buf.h[:, rc0:rc0 + CH], PBC.h[:], ALU.subtract))(),
              r=[r3buf.keys(rc0, rc0 + CH), PBC.keys()], w=[sD_a.keys()])
            A("vector", lambda e: e.tensor_tensor(sD_b.h[:], sD_a.h[:], PM.h[:], ALU.mult),
              r=[sD_a.keys(), PM.keys()], w=[sD_b.keys()])
            A("sync", (lambda s=s, oc=oc, c0=c0, c1=c1: lambda e: e.dma_start(
                out=outT[s, oc * 128:(oc + 1) * 128, c0:c1], in_=sD_b.h[:]))(),
              r=[sD_b.keys()], w=[(("out", (s, oc, tc)),)], dma="dout")

    def phaseD(s):
        A("gpsimd", wffn_dma, w=[vTa.keys(0, EC * 1024), (("vTa_ones", 0),)], dma="dwf")
        setA = {"t": sc_t, "c1": sc_c1, "n1": sc_n1, "w": sc_w, "sq": sc_sq, "ff": sc_ff}
        setB = {"t": sB_t, "c1": sB_c1, "n1": sB_n1, "w": sB_w, "sq": sB_sq, "ff": sB_ff}
        setC = {"t": sC_t, "c1": sC_c1, "n1": sC_n1, "w": sC_w, "sq": sC_sq, "ff": sC_ff}
        SETS = (setA, setB, setC)
        for tc in range(TC):
            ln_tc(s, "r2", tc)
            groups = ((0, 1, 2), (3, 4, 5), (6, 7))
            d3q = [(2 * g + o) for g in range(3) for o in range(3) if 2 * g + o < 8]
            nd3 = 0
            for gi, grp in enumerate(groups):
                streams = [chain_ops(s, tc, ec, SETS[j], ec % 4, ci=j)
                           for j, ec in enumerate(grp)]
                for i in range(max(len(o) for o in streams)):
                    for ops in streams:
                        if i < len(ops):
                            eng, fn, r, w, dma = ops[i]
                            A(eng, fn, r=r, w=w, dma=dma)
                if tc >= 1:
                    take = 3 if gi < 2 else 2
                    for _ in range(take):
                        d3_group(s, tc - 1, nd3)
                        nd3 += 1
            if tc >= 1:
                d4_block(s, tc - 1)
        for pp in range(4):
            d3_group(s, TC - 1, 2 * pp)
            d3_group(s, TC - 1, 2 * pp + 1)
        d4_block(s, TC - 1)

    for s in range(B):
        phaseA(s)
        phaseB(s)
        A("vector", lambda e: e.memset(ka.h[64:65, :], -1.0),
          w=[ka.keys(0, 2 * S), (("ka_neg1", 0),)])
        def wqk_load(p):
            nb = p % 2
            A("gpsimd", wqk_dma(wqk[nb], p, 0), w=[(("wqkh", nb, 0),)],
              dma="dwq00" if nb == 0 else "dwq10")
            A("gpsimd", wqk_dma(wqk[nb], p, 1), w=[(("wqkh", nb, 1),)],
              dma="dwq01" if nb == 0 else "dwq11")
        wqk_load(0)
        for p in range(H // 2):
            if p + 1 < H // 2:
                wqk_load(p + 1)
            phaseB2_pair(s, p, p % 2)
            stage_dmas(s, p)
            stage_norms(s, p)
            phaseC_pair(s, p)
        phaseD(s)

    counts, per_eng = sch.finalize()

    with es:
        sems = {c: es.enter_context(nc.semaphore("c_" + c)) for c in CHANNELS}
        with nc.Block() as block:
            def emit_stream(engobj, eng):
                watermark = {}
                for op in per_eng[eng]:
                    for chan, prod in op.waits.items():
                        val = prod.tok
                        if val is None:
                            raise RuntimeError(f"unassigned token (tag={prod.tag})")
                        if watermark.get(chan, -1) < val:
                            engobj.wait_ge(sems[chan], val)
                            watermark[chan] = val
                    instr = op.fn(engobj)
                    if op.needs_inc:
                        if instr is None:
                            raise RuntimeError(f"op returned no instruction (tag={op.tag})")
                        instr.then_inc(sems[op.chan], op.inc_k)

            @block.sync
            def _(sync):
                emit_stream(sync, "sync")
                if counts["dout"]:
                    sync.wait_ge(sems["dout"], counts["dout"])

            @block.gpsimd
            def _(g):
                emit_stream(g, "gpsimd")

            @block.vector
            def _(v):
                emit_stream(v, "vector")

            @block.scalar
            def _(s_):
                emit_stream(s_, "scalar")

            @block.tensor
            def _(t):
                emit_stream(t, "tensor")

    return nc


_NC_CACHE = [None]


def _host_stage(L0_attn_x, L0_attn_gamma, L0_attn_beta, L0_attn_w_qkv,
                L0_ffn_weight_fp8, L0_ffn_bias, L0_ffn_input_scale, L0_ffn_weight_scale):
    x = np.ascontiguousarray(np.asarray(L0_attn_x, dtype=np.float32))
    wqkvT = np.ascontiguousarray(np.asarray(L0_attn_w_qkv, dtype=np.float32).T)
    wffnT = np.ascontiguousarray(np.asarray(L0_ffn_weight_fp8, dtype=np.float32).T)
    gcol = np.ascontiguousarray(np.asarray(L0_attn_gamma, np.float32).reshape(EC, 128).T)
    bcol = np.ascontiguousarray(np.asarray(L0_attn_beta, np.float32).reshape(EC, 128).T)
    fbcol = np.ascontiguousarray(np.asarray(L0_ffn_bias, np.float32).reshape(EC, 128).T)
    iscale = np.ascontiguousarray(np.asarray(L0_ffn_input_scale, np.float32).reshape(1, 1))
    wscale = np.ascontiguousarray(np.asarray(L0_ffn_weight_scale, np.float32).reshape(1, 1))
    kl = np.arange(128)[:, None]
    ql = np.arange(128)[None, :]
    trimin = np.where(kl <= ql, np.float32(1e38), np.float32(-1e30)).astype(np.float32)
    in_maps = []
    for c in range(NCORES):
        xc = np.ascontiguousarray(x[2 * c:2 * c + 2].transpose(0, 2, 1))
        in_maps.append({
            "xT": xc, "wqkvT": wqkvT, "wffnT": wffnT,
            "gcol": gcol, "bcol": bcol, "fbcol": fbcol,
            "iscale": iscale, "wscale": wscale, "trimin": trimin,
        })
    return in_maps


def kernel(L0_attn_x, L0_attn_gamma, L0_attn_beta, L0_attn_w_qkv,
           L0_ffn_weight_fp8, L0_ffn_bias, L0_ffn_input_scale, L0_ffn_weight_scale):
    in_maps = _host_stage(L0_attn_x, L0_attn_gamma, L0_attn_beta, L0_attn_w_qkv,
                          L0_ffn_weight_fp8, L0_ffn_bias, L0_ffn_input_scale,
                          L0_ffn_weight_scale)
    if _NC_CACHE[0] is None:
        _NC_CACHE[0] = build_nc()
    nc = _NC_CACHE[0]
    res = run_bass_kernel_spmd(nc, in_maps, core_ids=list(range(NCORES)))
    outs = []
    for c in range(NCORES):
        o = np.asarray(res.results[c]["outT"])
        outs.append(np.ascontiguousarray(o.transpose(0, 2, 1)))
    return np.concatenate(outs, axis=0).astype(np.float32)


# version 22
# speedup vs baseline: 1.0070x; 1.0038x over previous
import numpy as np
import ml_dtypes

import concourse.bass as bass
import concourse.mybir as mybir
from concourse.bass_utils import run_bass_kernel_spmd

F32 = mybir.dt.float32
BF16 = mybir.dt.bfloat16
I32 = mybir.dt.int32
ALU = mybir.AluOpType
ACT = mybir.ActivationFunctionType
AX = mybir.AxisListType

DEBUG_SIM = False

B, S, E, H, D = 2, 2048, 1024, 16, 64
EC = 8
TC = 4
CH = 512
NCORES = 8
MASK_AND = -262144

ENGINES = ("sync", "gpsimd", "vector", "scalar", "tensor")
DMA_FAMS = ("dx0", "dx1", "dx2", "dx3", "dwq00", "dwq01", "dwq10", "dwq11", "dwv",
            "dwf", "dsh", "dcst", "dout", "dqa0", "dqa1", "dka0", "dka1", "dmr0", "dmr1")
CHANNELS = ("TE", "SE", "VE", "GP", "SY") + DMA_FAMS
ENG2CHAN = {"tensor": "TE", "scalar": "SE", "vector": "VE", "gpsimd": "GP", "sync": "SY"}
COMPUTE_CHANS = ("TE", "SE", "VE", "GP", "SY")


class Buf:
    def __init__(self, handle, name, slot_cols):
        self.h = handle
        self.name = name
        self.slot = slot_cols

    def keys(self, c0=None, c1=None):
        if c0 is None:
            return [(self.name, -1)]
        return [(self.name, i) for i in range(c0 // self.slot, (c1 - 1) // self.slot + 1)]


class Op:
    __slots__ = ("eng", "chan", "fn", "idx", "eseq", "inc_k", "needs_inc",
                 "tok", "waits", "tag")

    def __init__(self, eng, chan, fn, idx, inc_k, tag):
        self.eng, self.chan, self.fn = eng, chan, fn
        self.idx, self.inc_k = idx, inc_k
        self.eseq = None
        self.needs_inc = False
        self.tok = None
        self.waits = {}
        self.tag = tag


class Sched:
    def __init__(self):
        self.ops = []
        self.last_w = {}
        self.readers = {}
        self.eng_seq = {e: 0 for e in ENGINES}
        self.last_drain = {e: -1 for e in ENGINES}
        self._idx = 0

    def _need_drain(self, op_eng, prod):
        return prod.eseq > self.last_drain[op_eng]

    def add(self, eng, fn, r=(), w=(), dma=None, tag=""):
        chan = dma if dma else ENG2CHAN[eng]
        rk = [k for grp in r for k in grp]
        wk = [k for grp in w for k in grp]
        self._idx += 1
        op = Op(eng, chan, fn, self._idx, 16 if dma else 1, tag)
        preds = []
        for k in rk:
            p = self.last_w.get(k)
            if p is not None:
                preds.append(p)
        for k in wk:
            p = self.last_w.get(k)
            if p is not None:
                preds.append(p)
            preds.extend(self.readers.get(k, ()))
        need_drain = False
        for p in preds:
            if p is op:
                continue
            if (dma is None) and p.chan == chan and chan in COMPUTE_CHANS:
                if DEBUG_SIM and eng != "tensor" and self._need_drain(eng, p):
                    need_drain = True
            else:
                cur = op.waits.get(p.chan)
                if cur is None or p.idx > cur.idx:
                    op.waits[p.chan] = p
                p.needs_inc = True
        if need_drain:
            self._idx += 1
            d = Op(eng, ENG2CHAN[eng], lambda e: e.drain(), self._idx - 1, 1, "drain")
            d.eseq = self.eng_seq[eng]
            self.eng_seq[eng] += 1
            self.last_drain[eng] = d.eseq
            self.ops.append(d)
        op.eseq = self.eng_seq[eng]
        self.eng_seq[eng] += 1
        self.ops.append(op)
        for k in rk:
            self.readers.setdefault(k, []).append(op)
        for k in wk:
            self.last_w[k] = op
            self.readers[k] = []
        return op

    def finalize(self):
        counts = {c: 0 for c in CHANNELS}
        for op in self.ops:
            if op.chan == "dout":
                op.needs_inc = True
            if op.needs_inc:
                counts[op.chan] += op.inc_k
                op.tok = counts[op.chan]
        per_eng = {e: [] for e in ENGINES}
        for op in self.ops:
            per_eng[op.eng].append(op)
        return counts, per_eng


def build_nc():
    nc = bass.Bass("TRN2", target_bir_lowering=False,
                   detect_race_conditions=DEBUG_SIM)

    xT = nc.dram_tensor("xT", [B, E, S], F32, kind="ExternalInput").ap()
    wqkvT = nc.dram_tensor("wqkvT", [E, 3 * E], F32, kind="ExternalInput").ap()
    wffnT_d = nc.dram_tensor("wffnT", [E, E], F32, kind="ExternalInput").ap()
    gcol_d = nc.dram_tensor("gcol", [128, EC], F32, kind="ExternalInput").ap()
    bcol_d = nc.dram_tensor("bcol", [128, EC], F32, kind="ExternalInput").ap()
    fbcol_d = nc.dram_tensor("fbcol", [128, EC], F32, kind="ExternalInput").ap()
    iscale_d = nc.dram_tensor("iscale", [1, 1], F32, kind="ExternalInput").ap()
    wscale_d = nc.dram_tensor("wscale", [1, 1], F32, kind="ExternalInput").ap()
    trimin_d = nc.dram_tensor("trimin", [128, 128], F32, kind="ExternalInput").ap()
    outT = nc.dram_tensor("outT", [B, E, S], F32, kind="ExternalOutput").ap()

    from contextlib import ExitStack
    es = ExitStack()

    def sb(name, shape, dt, slot_cols):
        return Buf(es.enter_context(nc.sbuf_tensor(name, shape, dt)), name, slot_cols)

    def ps(name, shape):
        return Buf(es.enter_context(nc.psum_tensor(name, shape, F32)), name, 1 << 30)

    z1T = [sb(f"z1T{i}", [128, S], BF16, CH) for i in range(EC)]
    attnS = [sb(f"attnS{i}", [128, S], BF16, CH) for i in range(EC)]
    vTa = sb("vTa", [128, 16 * 16 * 65], BF16, 1040)
    pTA = sb("pTA", [128, 16 * CH], BF16, CH)
    pTB = sb("pTB", [128, 16 * CH], BF16, CH)
    q2 = sb("q2", [128, S], BF16, CH)
    k2 = sb("k2", [128, S], BF16, CH)
    wqk = [sb(f"wqk{i}", [128, EC * 256], BF16, 1 << 30) for i in range(2)]
    xsub = [sb(f"xsub{i}", [128, CH], F32, 1 << 30) for i in range(4)]
    tmpB = sb("tmpB", [128, S], BF16, CH)
    qa = sb("qa", [128, 2 * S], BF16, CH)
    ka = sb("ka", [128, 2 * S], BF16, CH)
    murow = sb("murow", [1, CH], F32, 1 << 30)
    rsrow = sb("rsrow", [1, CH], F32, 1 << 30)
    kn2row = sb("kn2row", [2, S], BF16, 1 << 30)
    kpm2row = sb("kpm2row", [2, S], BF16, 1 << 30)
    mrowb = sb("mrowb", [2, CH], BF16, 1 << 30)
    ones = sb("ones", [128, 128], F32, 1 << 30)
    bd2 = sb("bd2", [128, 2], BF16, 1 << 30)
    gcol = sb("gcol_s", [128, EC], F32, 1)
    bcol = sb("bcol_s", [128, EC], F32, 1)
    fbcol = sb("fbcol_s", [128, EC], F32, 1)
    sc_is = sb("sc_is", [1, 1], F32, 1 << 30)
    sc_ws = sb("sc_ws", [1, 1], F32, 1 << 30)
    sc_ri = sb("sc_ri", [1, 1], F32, 1 << 30)
    sc_sws = sb("sc_sws", [1, 1], F32, 1 << 30)
    ri_col = sb("ri_col", [128, 1], F32, 1 << 30)
    sws_col = sb("sws_col", [128, 1], F32, 1 << 30)
    c448col = sb("c448col", [128, 1], F32, 1 << 30)
    m448col = sb("m448col", [128, 1], F32, 1 << 30)
    ri2col = sb("ri2col", [128, 1], F32, 1 << 30)
    lnricol = sb("lnricol", [128, 1], F32, 1 << 30)
    lniscol = sb("lniscol", [128, 1], F32, 1 << 30)
    wscol = sb("wscol", [128, 1], F32, 1 << 30)
    sc_c448 = sb("sc_c448", [1, 1], F32, 1 << 30)
    sc_m448 = sb("sc_m448", [1, 1], F32, 1 << 30)
    sc_ri2 = sb("sc_ri2", [1, 1], F32, 1 << 30)
    sc_lnri = sb("sc_lnri", [1, 1], F32, 1 << 30)
    sc_lnis = sb("sc_lnis", [1, 1], F32, 1 << 30)
    trimin = sb("trimin_s", [128, 128], F32, 1 << 30)
    rct = sb("rct", [128, CH], F32, 1 << 30)
    sc_t = sb("sc_t", [128, CH], F32, 1 << 30)
    sc_w = sb("sc_w", [128, CH], F32, 1 << 30)
    sc_c1 = sb("sc_c1", [128, CH], F32, 1 << 30)
    sc_n1 = sb("sc_n1", [128, CH], F32, 1 << 30)
    sc_r2 = sb("sc_r2", [128, CH], F32, 1 << 30)
    sc_sq = sb("sc_sq", [128, CH], F32, 1 << 30)
    sc_ff = sb("sc_ff", [128, CH], F32, 1 << 30)
    qsq = tmpB
    epscol = sb("epscol", [128, 1], F32, 1 << 30)
    eps35col = sb("eps35col", [128, 1], F32, 1 << 30)

    class AliasBuf:
        __slots__ = ("h", "name", "_keys")
        def __init__(self, name, h, keys):
            self.name, self.h, self._keys = name, h, keys
        def keys(self, *a):
            return self._keys

    def alias_tile(nm, addr, under_keys):
        h = nc.virtual_sbuf_tensor(nm, [128, CH], F32)
        nc._tensor(h.name, h.shape, h.dtype, type="SB")
        mloc = nc.lookup_mloc(h)
        mloc.allocated = not nc.target_bir_lowering
        mloc.addr = addr
        return AliasBuf(nm, h, under_keys)

    _qa_addr = nc.lookup_mloc(qa.h).addr
    _ka_addr = nc.lookup_mloc(ka.h).addr
    sB_t = alias_tile("sB_t", _qa_addr + 0, qa.keys(0, 1024))
    sB_c1 = alias_tile("sB_c1", _qa_addr + 2048, qa.keys(1024, 2048))
    sB_n1 = alias_tile("sB_n1", _qa_addr + 4096, qa.keys(2048, 3072))
    sB_w = alias_tile("sB_w", _qa_addr + 6144, qa.keys(3072, 4096))
    sB_sq = alias_tile("sB_sq", _ka_addr + 0, ka.keys(0, 1024))
    sB_ff = alias_tile("sB_ff", _ka_addr + 2048, ka.keys(1024, 2048))
    sD_a = alias_tile("sD_a", _ka_addr + 4096, ka.keys(2048, 3072))
    sD_b = alias_tile("sD_b", _ka_addr + 6144, ka.keys(3072, 4096))
    _q2_addr = nc.lookup_mloc(q2.h).addr
    _k2_addr = nc.lookup_mloc(k2.h).addr
    _tmpB_addr = nc.lookup_mloc(tmpB.h).addr
    sC_t = alias_tile("sC_t", _q2_addr + 0, q2.keys(0, 1024))
    sC_c1 = alias_tile("sC_c1", _q2_addr + 2048, q2.keys(1024, 2048))
    sC_n1 = alias_tile("sC_n1", _k2_addr + 0, k2.keys(0, 1024))
    sC_w = alias_tile("sC_w", _k2_addr + 2048, k2.keys(1024, 2048))
    sC_sq = alias_tile("sC_sq", _tmpB_addr + 0, tmpB.keys(0, 1024))
    sC_ff = alias_tile("sC_ff", _tmpB_addr + 2048, tmpB.keys(1024, 2048))

    PW0 = ps("PW0", [128, 2 * CH])
    PW1 = ps("PW1", [128, 2 * CH])

    class _HalfH:
        __slots__ = ("th", "c0")
        def __init__(self, th, c0):
            self.th, self.c0 = th, c0
        def __getitem__(self, idx):
            if isinstance(idx, tuple):
                rows, cols = idx
            else:
                rows, cols = idx, slice(None)
            cs = cols.start or 0
            ce = cols.stop if cols.stop is not None else CH
            return self.th[rows, self.c0 + cs:self.c0 + ce]

    class _HalfBuf:
        __slots__ = ("h", "name")
        def __init__(self, pw, c0, name):
            self.h = _HalfH(pw.h, c0)
            self.name = name
        def keys(self, *a):
            return [(self.name, -1)]

    P0 = _HalfBuf(PW0, 0, "P0h")
    P1 = _HalfBuf(PW0, CH, "P1h")
    P2 = _HalfBuf(PW1, 0, "P2h")
    P3 = _HalfBuf(PW1, CH, "P3h")
    PAVA = ps("PAVA", [128, CH])
    PAVB = ps("PAVB", [128, CH])
    PBC = ps("PBC", [128, CH])
    PM = ps("PM", [128, CH])

    va3 = vTa.h[:].rearrange("p (t c) -> p t c", c=65)

    sch = Sched()
    A = sch.add

    A("sync", lambda e: e.dma_start(out=gcol.h[:], in_=gcol_d), w=[gcol.keys(0, EC), (("cstchain", 0),)], dma="dcst")
    A("sync", lambda e: e.dma_start(out=bcol.h[:], in_=bcol_d), w=[bcol.keys(0, EC), (("cstchain", 0),)], dma="dcst")
    A("sync", lambda e: e.dma_start(out=fbcol.h[:], in_=fbcol_d), w=[fbcol.keys(0, EC), (("cstchain", 0),)], dma="dcst")
    A("sync", lambda e: e.dma_start(out=sc_is.h[:], in_=iscale_d), w=[sc_is.keys(), (("cstchain", 0),)], dma="dcst")
    A("sync", lambda e: e.dma_start(out=sc_ws.h[:], in_=wscale_d), w=[sc_ws.keys(), (("cstchain", 0),)], dma="dcst")
    A("sync", lambda e: e.dma_start(out=trimin.h[:], in_=trimin_d), w=[trimin.keys(), (("cstchain", 0),)], dma="dcst")
    A("vector", lambda e: e.memset(ones.h[:], 1.0), w=[ones.keys()])
    A("vector", lambda e: e.memset(bd2.h[:], 0.0), w=[bd2.keys()])
    A("vector", lambda e: e.memset(bd2.h[0:64, 0:1], 1.0), w=[bd2.keys()])
    A("vector", lambda e: e.memset(bd2.h[64:128, 1:2], 1.0), w=[bd2.keys()])
    A("vector", lambda e: e.memset(epscol.h[:], 1e-38), w=[epscol.keys()])
    A("vector", lambda e: e.memset(eps35col.h[:], 1e-35), w=[eps35col.keys()])
    A("vector", lambda e: e.reciprocal(sc_ri.h[:], sc_is.h[:]), r=[sc_is.keys()], w=[sc_ri.keys()])
    A("vector", lambda e: e.tensor_tensor(sc_sws.h[:], sc_is.h[:], sc_ws.h[:], ALU.mult),
      r=[sc_is.keys(), sc_ws.keys()], w=[sc_sws.keys()])
    A("tensor", lambda e: e.matmul(PM.h[0:128, 0:1], ones.h[0:1, 0:128], sc_ri.h[0:1, 0:1],
                                   start=True, stop=True),
      r=[ones.keys(), sc_ri.keys()], w=[PM.keys()])
    A("vector", lambda e: e.tensor_copy(ri_col.h[:], PM.h[0:128, 0:1]),
      r=[PM.keys()], w=[ri_col.keys()])
    A("tensor", lambda e: e.matmul(PM.h[0:128, 0:1], ones.h[0:1, 0:128], sc_sws.h[0:1, 0:1],
                                   start=True, stop=True),
      r=[ones.keys(), sc_sws.keys()], w=[PM.keys()])
    A("vector", lambda e: e.tensor_copy(sws_col.h[:], PM.h[0:128, 0:1]),
      r=[PM.keys()], w=[sws_col.keys()])
    A("vector", lambda e: e.tensor_scalar(sc_c448.h[:], sc_is.h[:], 448.0, None, ALU.mult),
      r=[sc_is.keys()], w=[sc_c448.keys()])
    A("vector", lambda e: e.tensor_scalar(sc_m448.h[:], sc_is.h[:], -448.0, None, ALU.mult),
      r=[sc_is.keys()], w=[sc_m448.keys()])
    A("vector", lambda e: e.tensor_tensor(sc_ri2.h[:], sc_ri.h[:], sc_ri.h[:], ALU.mult),
      r=[sc_ri.keys()], w=[sc_ri2.keys()])
    A("scalar", lambda e: e.activation(sc_lnri.h[:], sc_ri.h[:], ACT.Ln),
      r=[sc_ri.keys()], w=[sc_lnri.keys()])
    A("scalar", lambda e: e.activation(sc_lnis.h[:], sc_is.h[:], ACT.Ln),
      r=[sc_is.keys()], w=[sc_lnis.keys()])
    for _src, _dst in ((sc_c448, c448col), (sc_m448, m448col), (sc_ri2, ri2col),
                       (sc_lnri, lnricol), (sc_lnis, lniscol), (sc_ws, wscol)):
        A("tensor", (lambda _src=_src: lambda e: e.matmul(
            PM.h[0:128, 0:1], ones.h[0:1, 0:128], _src.h[0:1, 0:1],
            start=True, stop=True))(),
          r=[ones.keys(), _src.keys()], w=[PM.keys()])
        A("vector", (lambda _dst=_dst: lambda e: e.tensor_copy(
            _dst.h[:], PM.h[0:128, 0:1]))(),
          r=[PM.keys()], w=[_dst.keys()])

    def x_dma(i, s, ec, tc):
        return lambda e: e.dma_start(
            out=xsub[i].h[:], in_=xT[s, ec * 128:(ec + 1) * 128, tc * CH:(tc + 1) * CH])

    def wqk_dma(buf, p, j):
        c0 = 128 * p if j == 0 else E + 128 * p
        def fn(e):
            src = wqkvT[:, c0:c0 + 128].rearrange("(c p) n -> p c n", p=128)
            dst = buf.h[:].rearrange("p (c n) -> p c n", n=256)[:, :, j * 128:(j + 1) * 128]
            return e.dma_start(out=dst, in_=src)
        return fn

    def wv_dma(e):
        src = wqkvT[:, 2 * E:3 * E].rearrange("(c p) n -> p c n", p=128)
        dst = pTA.h[:].rearrange("p (c n) -> p c n", n=1024)
        return e.dma_start(out=dst, in_=src)

    def wffn_dma(e):
        src = wffnT_d.rearrange("(c p) n -> p c n", p=128)
        dst = vTa.h[:, 0:EC * 1024].rearrange("p (c n) -> p c n", n=1024)
        return e.dma_start(out=dst, in_=src)

    def ln_tc(s, src_kind, tc):
        xi = (tc * EC) % 4
        for ec in range(EC):
            i = xi
            xk = (("xsub", i),)
            A("sync", x_dma(i, s, ec, tc), w=[xk], dma=f"dx{i}")
            if src_kind == "x":
                srcb, srck = xsub[i].h, [xk]
            else:
                A("vector", (lambda i=i, ec=ec, tc=tc: lambda e: e.tensor_tensor(
                    sc_r2.h[:], xsub[i].h[:], attnS[ec].h[:, tc * CH:(tc + 1) * CH],
                    ALU.add))(),
                  r=[xk, attnS[ec].keys(tc * CH, (tc + 1) * CH)], w=[sc_r2.keys()])
                srcb, srck = sc_r2.h, [sc_r2.keys()]
            A("scalar", (lambda srcb=srcb: lambda e: e.activation(
                rct.h[:], srcb[:], ACT.Square))(), r=srck, w=[rct.keys()])
            A("tensor", (lambda srcb=srcb, ec=ec: lambda e: e.matmul(
                P2.h[0:1, :], ones.h[0:128, 0:1], srcb[:],
                start=(ec == 0), stop=(ec == EC - 1)))(),
              r=srck + [ones.keys()], w=[P2.keys()])
            A("tensor", (lambda ec=ec: lambda e: e.matmul(
                P3.h[0:1, :], ones.h[0:128, 0:1], rct.h[:],
                start=(ec == 0), stop=(ec == EC - 1)))(),
              r=[rct.keys(), ones.keys()], w=[P3.keys()])
            xi = (xi + 1) % 4
        A("scalar", lambda e: e.activation(murow.h[:], P2.h[0:1, :], ACT.Copy, scale=1.0 / E),
          r=[P2.keys()], w=[murow.keys()])
        A("scalar", lambda e: e.activation(rsrow.h[:], P3.h[0:1, :], ACT.Copy, scale=1.0 / E),
          r=[P3.keys()], w=[rsrow.keys()])
        A("vector", lambda e: e.tensor_tensor(PM.h[0:1, :], murow.h[:], murow.h[:], ALU.mult),
          r=[murow.keys()], w=[PM.keys()])
        A("vector", lambda e: e.tensor_tensor(rsrow.h[:], rsrow.h[:], PM.h[0:1, :], ALU.subtract),
          r=[PM.keys(), rsrow.keys()], w=[rsrow.keys()])
        A("vector", lambda e: e.tensor_scalar(rsrow.h[:], rsrow.h[:], 1e-5, None, ALU.add),
          r=[rsrow.keys()], w=[rsrow.keys()])
        A("scalar", lambda e: e.activation(rsrow.h[:], rsrow.h[:], ACT.Ln),
          r=[rsrow.keys()], w=[rsrow.keys()])
        A("scalar", lambda e: e.activation(rsrow.h[:], rsrow.h[:], ACT.Exp, scale=-0.5),
          r=[rsrow.keys()], w=[rsrow.keys()])
        A("tensor", lambda e: e.matmul(P0.h[:], ones.h[0:1, 0:128], murow.h[:],
                                       start=True, stop=True),
          r=[ones.keys(), murow.keys()], w=[P0.keys()])
        A("tensor", lambda e: e.matmul(P1.h[:], ones.h[0:1, 0:128], rsrow.h[:],
                                       start=True, stop=True),
          r=[ones.keys(), rsrow.keys()], w=[P1.keys()])

    def phaseA(s):
        for tc in range(TC):
            ln_tc(s, "x", tc)
            c0, c1 = tc * CH, (tc + 1) * CH
            xi = (tc * EC) % 4
            for ec in range(EC):
                i = xi
                xk = (("xsub", i),)
                A("sync", x_dma(i, s, ec, tc), w=[xk], dma=f"dx{i}")
                A("vector", (lambda i=i: lambda e: e.tensor_tensor(
                    sc_t.h[:], xsub[i].h[:], P0.h[:], ALU.subtract))(),
                  r=[xk, P0.keys()], w=[sc_t.keys()])
                A("vector", lambda e: e.tensor_tensor(sc_w.h[:], sc_t.h[:], P1.h[:], ALU.mult),
                  r=[sc_t.keys(), P1.keys()], w=[sc_w.keys()])
                A("vector", (lambda ec=ec, c0=c0, c1=c1: lambda e: e.tensor_scalar(
                    z1T[ec].h[:, c0:c1], sc_w.h[:], gcol.h[:, ec:ec + 1],
                    bcol.h[:, ec:ec + 1], ALU.mult, ALU.add))(),
                  r=[sc_w.keys(), gcol.keys(0, EC), bcol.keys(0, EC)],
                  w=[z1T[ec].keys(c0, c1)])
                xi = (xi + 1) % 4

    def phaseB(s):
        A("gpsimd", wv_dma, w=[pTA.keys(0, 16 * CH)], dma="dwv")
        A("vector", lambda e: e.memset(va3[:, :, 64:65], 1.0),
          w=[vTa.keys(0, 16 * 16 * 65), (("vTa_ones", 0),)])
        for tt in range(16):
            for half in range(2):
                PT = P2 if half == 0 else P3
                for ec in range(EC):
                    A("tensor", (lambda tt=tt, half=half, ec=ec, PT=PT: lambda e: e.matmul(
                        PT.h[:], z1T[ec].h[:, tt * 128:(tt + 1) * 128],
                        pTA.h[:, ec * 1024 + half * CH:ec * 1024 + (half + 1) * CH],
                        start=(ec == 0), stop=(ec == EC - 1)))(),
                      r=[z1T[ec].keys(tt * 128, (tt + 1) * 128),
                         pTA.keys(ec * 1024 + half * CH, ec * 1024 + (half + 1) * CH)],
                      w=[PT.keys()])
                h0 = half * 8
                A("vector", (lambda tt=tt, h0=h0, PT=PT: lambda e: e.tensor_copy(
                    va3[:, tt * 16 + h0:tt * 16 + h0 + 8, 0:64],
                    PT.h[:].rearrange("p (h d) -> p h d", d=64)))(),
                  r=[PT.keys(), (("vTa_ones", 0),)],
                  w=[vTa.keys(tt * 1040, (tt + 1) * 1040)])

    def phaseB2_pair(s, p, wbuf):
        for tc in range(TC):
            phaseB2_tc(s, p, wbuf, tc)

    def phaseB2_tc(s, p, wbuf, tc):
        if True:
            c0, c1 = tc * CH, (tc + 1) * CH
            PQ = P0 if tc % 2 == 0 else P1
            PK = P2 if tc % 2 == 0 else P3
            for ec in range(EC):
                A("tensor", (lambda ec=ec, c0=c0, c1=c1, PQ=PQ, wbuf=wbuf: lambda e: e.matmul(
                    PQ.h[:], wqk[wbuf].h[:, ec * 256:ec * 256 + 128], z1T[ec].h[:, c0:c1],
                    start=(ec == 0), stop=(ec == EC - 1)))(),
                  r=[(("wqkh", wbuf, 0), ("wqkh", wbuf, 1)), z1T[ec].keys(c0, c1)], w=[PQ.keys()])
            for ec in range(EC):
                A("tensor", (lambda ec=ec, c0=c0, c1=c1, PK=PK, wbuf=wbuf: lambda e: e.matmul(
                    PK.h[:], wqk[wbuf].h[:, ec * 256 + 128:ec * 256 + 256],
                    z1T[ec].h[:, c0:c1],
                    start=(ec == 0), stop=(ec == EC - 1)))(),
                  r=[(("wqkh", wbuf, 0), ("wqkh", wbuf, 1)), z1T[ec].keys(c0, c1)], w=[PK.keys()])
            A("scalar", (lambda c0=c0, c1=c1, PQ=PQ: lambda e: e.activation(
                q2.h[:, c0:c1], PQ.h[:], ACT.Copy, scale=0.125))(),
              r=[PQ.keys()], w=[q2.keys(c0, c1)])
            A("vector", (lambda c0=c0, c1=c1, PK=PK: lambda e: e.tensor_copy(
                k2.h[:, c0:c1], PK.h[:]))(),
              r=[PK.keys()], w=[k2.keys(c0, c1)])

    def stage_dmas(s, p):
        A("sync", lambda e: e.dma_start(out=qa.h[0:64, 0:S], in_=q2.h[0:64, :]),
          r=[q2.keys(0, S)], w=[qa.keys(0, S)], dma="dqa0")
        A("sync", lambda e: e.dma_start(out=qa.h[0:64, S:2 * S], in_=q2.h[64:128, :]),
          r=[q2.keys(0, S)], w=[qa.keys(S, 2 * S)], dma="dqa1")
        A("sync", lambda e: e.dma_start(out=ka.h[0:64, 0:S], in_=k2.h[0:64, :]),
          r=[k2.keys(0, S)], w=[ka.keys(0, S)], dma="dka0")
        A("sync", lambda e: e.dma_start(out=ka.h[0:64, S:2 * S], in_=k2.h[64:128, :]),
          r=[k2.keys(0, S)], w=[ka.keys(S, 2 * S)], dma="dka1")

    def stage_norms(s, p):
        A("gpsimd", lambda e: e.tensor_tensor(qsq.h[:], k2.h[:], k2.h[:], ALU.mult),
          r=[k2.keys(0, S)], w=[qsq.keys(0, S)])
        for tc in range(TC):
            c0, c1 = tc * CH, (tc + 1) * CH
            A("tensor", (lambda c0=c0, c1=c1: lambda e: e.matmul(
                PM.h[0:2, :], bd2.h[:], qsq.h[:, c0:c1], start=True, stop=True))(),
              r=[bd2.keys(), qsq.keys(c0, c1)], w=[PM.keys()])
            A("vector", (lambda c0=c0: lambda e: e.tensor_copy(
                kn2row.h[0:2, c0:c0 + CH], PM.h[0:2, :]))(),
              r=[PM.keys()], w=[(("kn2", c0),)])
        A("vector", lambda e: e.tensor_tensor_scan(
            kpm2row.h[0:2, :], kn2row.h[0:2, :], kn2row.h[0:2, :], 0.0, ALU.max, ALU.max),
          r=[tuple(("kn2", c) for c in range(0, S, CH))], w=[(("kpm", 0),)])
        A("vector", lambda e: e.tensor_scalar(kpm2row.h[0:2, :], kpm2row.h[0:2, :],
                                              1.004, None, ALU.mult),
          r=[(("kpm", 0),)], w=[(("kpm", 0),)])
        A("gpsimd", lambda e: e.tensor_tensor(qsq.h[:], q2.h[:], q2.h[:], ALU.mult),
          r=[q2.keys(0, S)], w=[qsq.keys(0, S)])
        for tc in range(TC):
            c0, c1 = tc * CH, (tc + 1) * CH
            A("tensor", (lambda c0=c0, c1=c1: lambda e: e.matmul(
                PM.h[0:2, :], bd2.h[:], qsq.h[:, c0:c1], start=True, stop=True))(),
              r=[bd2.keys(), qsq.keys(c0, c1)], w=[PM.keys()])
            A("vector", (lambda c0=c0: lambda e: e.tensor_tensor(
                rct.h[0:2, :], PM.h[0:2, :], kpm2row.h[0:2, c0:c0 + CH], ALU.mult))(),
              r=[PM.keys(), (("kpm", 0),)], w=[rct.keys()])
            A("scalar", lambda e: e.activation(rct.h[0:2, :], rct.h[0:2, :], ACT.Ln,
                                               bias=eps35col.h[0:2, 0:1]),
              r=[rct.keys(), eps35col.keys()], w=[rct.keys()])
            A("scalar", lambda e: e.activation(mrowb.h[:], rct.h[0:2, :], ACT.Exp, scale=0.5),
              r=[rct.keys()], w=[mrowb.keys()])
            A("sync", (lambda c0=c0: lambda e: e.dma_start(
                out=qa.h[64:65, c0:c0 + CH], in_=mrowb.h[0:1, :]))(),
              r=[mrowb.keys()], w=[(("qamr", 0, c0 // CH),)], dma="dmr0")
            A("sync", (lambda c0=c0: lambda e: e.dma_start(
                out=qa.h[64:65, S + c0:S + c0 + CH], in_=mrowb.h[1:2, :]))(),
              r=[mrowb.keys()], w=[(("qamr", 1, c0 // CH),)], dma="dmr1")

    def phaseC_pair(s, p, hook=None):
        hA, hB = 2 * p, 2 * p + 1
        for qc in range(TC):
            q0, q1 = qc * CH, (qc + 1) * CH
            nkt = 4 * qc + 4
            def sc_mm(head, kt, PH):
                off = 0 if head == 0 else S
                mr = (("qamr", head, qc),)
                A("tensor", (lambda kt=kt, q0=q0, q1=q1, off=off, PH=PH, mr=mr: lambda e: e.matmul(
                    PH.h[:], ka.h[0:65, off + kt * 128:off + (kt + 1) * 128],
                    qa.h[0:65, off + q0:off + q1], start=True, stop=True))(),
                  r=[ka.keys(off + kt * 128, off + (kt + 1) * 128), (("ka_neg1", 0),),
                     qa.keys(off + q0, off + q1), mr], w=[PH.keys()])

            nplain = 4 * qc
            for kt in range(nplain):
                PA = P0 if kt % 2 == 0 else P1
                PB = P2 if kt % 2 == 0 else P3
                sc_mm(0, kt, PA)
                sc_mm(1, kt, PB)
                A("scalar", (lambda kt=kt, PA=PA: lambda e: e.activation(
                    pTA.h[:, kt * CH:(kt + 1) * CH], PA.h[:], ACT.Exp))(),
                  r=[PA.keys()], w=[pTA.keys(kt * CH, (kt + 1) * CH)])
                A("scalar", (lambda kt=kt, PB=PB: lambda e: e.activation(
                    pTB.h[:, kt * CH:(kt + 1) * CH], PB.h[:], ACT.Exp))(),
                  r=[PB.keys()], w=[pTB.keys(kt * CH, (kt + 1) * CH)])
            for kt in range(nplain, nkt):
                PA = P0 if kt % 2 == 0 else P1
                PB = P2 if kt % 2 == 0 else P3
                jo = 128 * (kt - 4 * qc)
                sc_mm(0, kt, PA)
                sc_mm(1, kt, PB)
                A("vector", (lambda jo=jo, PA=PA: lambda e: e.tensor_tensor(
                    PA.h[:, jo:jo + 128], PA.h[:, jo:jo + 128], trimin.h[:], ALU.min))(),
                  r=[PA.keys(), trimin.keys()], w=[PA.keys()])
                A("vector", (lambda jo=jo, PB=PB: lambda e: e.tensor_tensor(
                    PB.h[:, jo:jo + 128], PB.h[:, jo:jo + 128], trimin.h[:], ALU.min))(),
                  r=[PB.keys(), trimin.keys()], w=[PB.keys()])
                A("scalar", (lambda kt=kt, jo=jo, PA=PA: lambda e: e.activation(
                    pTA.h[:, kt * CH + jo:(kt + 1) * CH], PA.h[:, jo:], ACT.Exp))(),
                  r=[PA.keys()], w=[pTA.keys(kt * CH, (kt + 1) * CH)])
                A("scalar", (lambda kt=kt, jo=jo, PB=PB: lambda e: e.activation(
                    pTB.h[:, kt * CH + jo:(kt + 1) * CH], PB.h[:, jo:], ACT.Exp))(),
                  r=[PB.keys()], w=[pTB.keys(kt * CH, (kt + 1) * CH)])
                if jo > 0:
                    A("gpsimd", (lambda kt=kt, jo=jo: lambda e: e.memset(
                        pTA.h[:, kt * CH:kt * CH + jo], 0.0))(),
                      w=[pTA.keys(kt * CH, (kt + 1) * CH)])
                    A("gpsimd", (lambda kt=kt, jo=jo: lambda e: e.memset(
                        pTB.h[:, kt * CH:kt * CH + jo], 0.0))(),
                      w=[pTB.keys(kt * CH, (kt + 1) * CH)])
            for kt in range(nkt):
                A("tensor", (lambda kt=kt, hA=hA, nkt=nkt: lambda e: e.matmul(
                    PAVA.h[0:65, :], vTa.h[:, (kt * 16 + hA) * 65:(kt * 16 + hA) * 65 + 65],
                    pTA.h[:, kt * CH:(kt + 1) * CH],
                    start=(kt == 0), stop=(kt == nkt - 1)))(),
                  r=[vTa.keys(kt * 1040, (kt + 1) * 1040), (("vTa_ones", 0),),
                     pTA.keys(kt * CH, (kt + 1) * CH)],
                  w=[PAVA.keys()])
            for kt in range(nkt):
                A("tensor", (lambda kt=kt, hB=hB, nkt=nkt: lambda e: e.matmul(
                    PAVB.h[0:65, :], vTa.h[:, (kt * 16 + hB) * 65:(kt * 16 + hB) * 65 + 65],
                    pTB.h[:, kt * CH:(kt + 1) * CH],
                    start=(kt == 0), stop=(kt == nkt - 1)))(),
                  r=[vTa.keys(kt * 1040, (kt + 1) * 1040), (("vTa_ones", 0),),
                     pTB.keys(kt * CH, (kt + 1) * CH)],
                  w=[PAVB.keys()])
            A("scalar", lambda e: e.activation(rct.h[64:65, :], PAVA.h[64:65, :], ACT.Ln),
              r=[PAVA.keys()], w=[rct.keys()])
            A("scalar", lambda e: e.activation(rct.h[64:65, :], rct.h[64:65, :], ACT.Exp,
                                               scale=-1.0),
              r=[rct.keys()], w=[rct.keys()])
            A("tensor", lambda e: e.matmul(PBC.h[0:64, :], ones.h[64:65, 0:64], rct.h[64:65, :],
                                           start=True, stop=True),
              r=[ones.keys(), rct.keys()], w=[PBC.keys()])
            A("scalar", lambda e: e.activation(sc_c1.h[0:64, :], PBC.h[0:64, :], ACT.Copy),
              r=[PBC.keys()], w=[sc_c1.keys()])
            A("scalar", lambda e: e.activation(sc_n1.h[0:64, :], PAVA.h[0:64, :], ACT.Copy),
              r=[PAVA.keys()], w=[sc_n1.keys()])
            A("vector", (lambda p=p, q0=q0, q1=q1: lambda e: e.tensor_tensor(
                attnS[p].h[0:64, q0:q1], sc_n1.h[0:64, :], sc_c1.h[0:64, :], ALU.mult))(),
              r=[sc_n1.keys(), sc_c1.keys()], w=[attnS[p].keys(q0, q1)])
            A("scalar", lambda e: e.activation(rct.h[64:65, :], PAVB.h[64:65, :], ACT.Ln),
              r=[PAVB.keys()], w=[rct.keys()])
            A("scalar", lambda e: e.activation(rct.h[64:65, :], rct.h[64:65, :], ACT.Exp,
                                               scale=-1.0),
              r=[rct.keys()], w=[rct.keys()])
            A("tensor", lambda e: e.matmul(PBC.h[0:64, :], ones.h[64:65, 0:64], rct.h[64:65, :],
                                           start=True, stop=True),
              r=[ones.keys(), rct.keys()], w=[PBC.keys()])
            A("scalar", lambda e: e.activation(sc_c1.h[0:64, :], PBC.h[0:64, :], ACT.Copy),
              r=[PBC.keys()], w=[sc_c1.keys()])
            A("scalar", lambda e: e.activation(sc_n1.h[0:64, :], PAVB.h[0:64, :], ACT.Copy),
              r=[PAVB.keys()], w=[sc_n1.keys()])
            A("vector", (lambda q0=q0, q1=q1: lambda e: e.tensor_tensor(
                tmpB.h[0:64, q0:q1], sc_n1.h[0:64, :], sc_c1.h[0:64, :], ALU.mult))(),
              r=[sc_n1.keys(), sc_c1.keys()], w=[tmpB.keys(q0, q1)])
            if hook is not None:
                hook(qc)
        A("gpsimd", (lambda p=p: lambda e: e.dma_start(
            out=attnS[p].h[64:128, :], in_=tmpB.h[0:64, :]))(),
          r=[tmpB.keys(0, S)], w=[attnS[p].keys(0, S), (("shiftchain", 0),)], dma="dsh")

    LN2C = 0.6931471805599453

    def chain_ops(s, tc, ec, SC, xi, ci=0):
        c0, c1 = tc * CH, (tc + 1) * CH
        t, cc, n1, w_, sq, ff = SC["t"], SC["c1"], SC["n1"], SC["w"], SC["sq"], SC["ff"]
        ak = attnS[ec].keys(c0, c1)
        xk = (("xsub", xi),)
        ops = []
        def O(eng, fn, r=(), w=(), dma=None):
            ops.append((eng, fn, list(r), list(w), dma))
        O("sync", x_dma(xi, s, ec, tc), w=[xk], dma=f"dx{xi}")
        O("vector", (lambda xi=xi, ec=ec, c0=c0, c1=c1, t=t: lambda e: e.tensor_tensor(
            t.h[:], xsub[xi].h[:], attnS[ec].h[:, c0:c1], ALU.add))(),
          r=[xk, ak], w=[t.keys()])
        O("vector", (lambda t=t, cc=cc: lambda e: e.tensor_tensor(
            cc.h[:], t.h[:], P0.h[:], ALU.subtract))(),
          r=[t.keys(), P0.keys()], w=[cc.keys()])
        O("vector", (lambda cc=cc, w_=w_: lambda e: e.tensor_tensor(
            w_.h[:], cc.h[:], P1.h[:], ALU.mult))(),
          r=[cc.keys(), P1.keys()], w=[w_.keys()])
        O("scalar", (lambda ec=ec, c0=c0, c1=c1, w_=w_: lambda e: e.activation(
            attnS[ec].h[:, c0:c1], w_.h[:], ACT.Copy))(),
          r=[w_.keys()], w=[ak])
        O("vector", (lambda t=t, w_=w_: lambda e: e.tensor_scalar(
            t.h[:], w_.h[:], c448col.h[:, 0:1], m448col.h[:, 0:1], ALU.min, ALU.max))(),
          r=[w_.keys(), c448col.keys(), m448col.keys()], w=[t.keys()])
        O("gpsimd", (lambda t=t, cc=cc: lambda e: e.tensor_tensor(
            cc.h[:], t.h[:], t.h[:], ALU.mult))(), r=[t.keys()], w=[cc.keys()])
        O("scalar", (lambda cc=cc: lambda e: e.activation(
            cc.h[:], cc.h[:], ACT.Ln, scale=ri2col.h[:, 0:1], bias=epscol.h[:, 0:1]))(),
          r=[cc.keys(), ri2col.keys(), epscol.keys()], w=[cc.keys()])
        O("vector", (lambda cc=cc: lambda e: e.tensor_scalar(
            cc.h[:], cc.h[:], 0.7213475204444817, -6.0, ALU.mult, ALU.max))(),
          r=[cc.keys()], w=[cc.keys()])
        O("vector", (lambda cc=cc, ff=ff: lambda e: e.tensor_scalar(
            ff.h[:], cc.h[:], 2.0, -8.0, ALU.is_ge, ALU.mult))(),
          r=[cc.keys()], w=[ff.keys()])
        O("vector", (lambda cc=cc, ff=ff, w_=w_: lambda e: e.scalar_tensor_tensor(
            w_.h[:], cc.h[:], 6.0, ff.h[:], ALU.add, ALU.add))(),
          r=[cc.keys(), ff.keys()], w=[w_.keys()])
        for li, L in enumerate((4.0, 2.0, 1.0)):
            peng = "vector" if (li + ci) % 4 == 0 else "gpsimd"
            O("vector", (lambda L=L, w_=w_, ff=ff: lambda e: e.tensor_scalar(
                ff.h[:], w_.h[:], L, -L, ALU.is_ge, ALU.mult))(),
              r=[w_.keys()], w=[ff.keys()])
            O(peng, (lambda w_=w_, ff=ff: lambda e: e.tensor_tensor(
                w_.h[:], w_.h[:], ff.h[:], ALU.add))(),
              r=[w_.keys(), ff.keys()], w=[w_.keys()])
        O("vector", (lambda cc=cc, w_=w_, n1=n1: lambda e: e.scalar_tensor_tensor(
            n1.h[:], cc.h[:], 6.0, w_.h[:], ALU.add, ALU.subtract))(),
          r=[cc.keys(), w_.keys()], w=[n1.keys()])
        O("vector", (lambda n1=n1: lambda e: e.tensor_scalar(
            n1.h[:], n1.h[:], -LN2C, 9.0 * LN2C, ALU.mult, ALU.add))(),
          r=[n1.keys()], w=[n1.keys()])
        O("scalar", (lambda n1=n1, cc=cc: lambda e: e.activation(
            cc.h[:], n1.h[:], ACT.Exp, bias=lnricol.h[:, 0:1]))(),
          r=[n1.keys(), lnricol.keys()], w=[cc.keys()])
        O("scalar", (lambda n1=n1, sq=sq: lambda e: e.activation(
            sq.h[:], n1.h[:], ACT.Exp, scale=-1.0, bias=lniscol.h[:, 0:1]))(),
          r=[n1.keys(), lniscol.keys()], w=[sq.keys()])
        O("gpsimd", (lambda t=t, cc=cc: lambda e: e.tensor_tensor(
            t.h[:], t.h[:], cc.h[:], ALU.mult))(),
          r=[t.keys(), cc.keys()], w=[t.keys()])
        O("vector", (lambda t=t, ff=ff: lambda e: e.tensor_scalar(
            ff.h[:], t.h[:], 15.5, -32.0, ALU.is_ge, ALU.mult))(),
          r=[t.keys()], w=[ff.keys()])
        O("vector", (lambda t=t, ff=ff, w_=w_: lambda e: e.scalar_tensor_tensor(
            w_.h[:], t.h[:], 16.5, ff.h[:], ALU.add, ALU.add))(),
          r=[t.keys(), ff.keys()], w=[w_.keys()])
        for li, L in enumerate((16.0, 8.0, 4.0, 2.0, 1.0)):
            peng = "vector" if (li + 3 + ci) % 4 == 0 else "gpsimd"
            O("vector", (lambda L=L, w_=w_, ff=ff: lambda e: e.tensor_scalar(
                ff.h[:], w_.h[:], L, -L, ALU.is_ge, ALU.mult))(),
              r=[w_.keys()], w=[ff.keys()])
            O(peng, (lambda w_=w_, ff=ff: lambda e: e.tensor_tensor(
                w_.h[:], w_.h[:], ff.h[:], ALU.add))(),
              r=[w_.keys(), ff.keys()], w=[w_.keys()])
        O("vector", (lambda t=t, w_=w_: lambda e: e.scalar_tensor_tensor(
            t.h[:], t.h[:], 0.5, w_.h[:], ALU.add, ALU.subtract))(),
          r=[t.keys(), w_.keys()], w=[t.keys()])
        O("vector", (lambda ec=ec, c0=c0, c1=c1, t=t, sq=sq: lambda e: e.tensor_tensor(
            z1T[ec].h[:, c0:c1], t.h[:], sq.h[:], ALU.mult))(),
          r=[t.keys(), sq.keys()], w=[z1T[ec].keys(c0, c1)])
        return ops

    def d3_group(s, tc, oc):
        c0, c1 = tc * CH, (tc + 1) * CH
        PF = PAVA if oc % 2 == 0 else PAVB
        for ec in range(EC):
            A("tensor", (lambda ec=ec, oc=oc, c0=c0, c1=c1, PF=PF: lambda e: e.matmul(
                PF.h[:], vTa.h[:, ec * 1024 + oc * 128:ec * 1024 + (oc + 1) * 128],
                z1T[ec].h[:, c0:c1], start=(ec == 0), stop=(ec == EC - 1)))(),
              r=[vTa.keys(ec * 1024, ec * 1024 + 1024), z1T[ec].keys(c0, c1)],
              w=[PF.keys()])
        A("scalar", (lambda oc=oc, PF=PF: lambda e: e.activation(
            sD_a.h[:], PF.h[:], ACT.Copy, scale=wscol.h[:, 0:1]))(),
          r=[PF.keys(), wscol.keys()], w=[sD_a.keys()])
        A("vector", (lambda oc=oc, c0=c0, c1=c1: lambda e: e.scalar_tensor_tensor(
            sD_b.h[:], sD_a.h[:], fbcol.h[:, oc:oc + 1], attnS[oc].h[:, c0:c1],
            ALU.add, ALU.add))(),
          r=[sD_a.keys(), fbcol.keys(0, EC), attnS[oc].keys(c0, c1)], w=[sD_b.keys()])
        r3buf = pTA if oc < 4 else pTB
        rc0 = (oc % 4) * S + c0
        A("scalar", (lambda r3buf=r3buf, rc0=rc0: lambda e: e.activation(
            r3buf.h[:, rc0:rc0 + CH], sD_b.h[:], ACT.Copy))(),
          r=[sD_b.keys()], w=[r3buf.keys(rc0, rc0 + CH)])
        A("scalar", lambda e: e.activation(sD_a.h[:], sD_b.h[:], ACT.Square),
          r=[sD_b.keys()], w=[sD_a.keys()])
        A("tensor", (lambda oc=oc: lambda e: e.matmul(
            P2.h[0:1, :], ones.h[0:128, 0:1], sD_b.h[:],
            start=(oc == 0), stop=(oc == EC - 1)))(),
          r=[sD_b.keys(), ones.keys()], w=[P2.keys()])
        A("tensor", (lambda oc=oc: lambda e: e.matmul(
            P3.h[0:1, :], ones.h[0:128, 0:1], sD_a.h[:],
            start=(oc == 0), stop=(oc == EC - 1)))(),
          r=[sD_a.keys(), ones.keys()], w=[P3.keys()])

    def d4_block(s, tc):
        c0, c1 = tc * CH, (tc + 1) * CH
        A("scalar", lambda e: e.activation(murow.h[:], P2.h[0:1, :], ACT.Copy, scale=1.0 / E),
          r=[P2.keys()], w=[murow.keys()])
        A("scalar", lambda e: e.activation(rsrow.h[:], P3.h[0:1, :], ACT.Copy, scale=1.0 / E),
          r=[P3.keys()], w=[rsrow.keys()])
        A("vector", lambda e: e.tensor_tensor(PM.h[0:1, :], murow.h[:], murow.h[:], ALU.mult),
          r=[murow.keys()], w=[PM.keys()])
        A("vector", lambda e: e.tensor_tensor(rsrow.h[:], rsrow.h[:], PM.h[0:1, :], ALU.subtract),
          r=[PM.keys(), rsrow.keys()], w=[rsrow.keys()])
        A("vector", lambda e: e.tensor_scalar(rsrow.h[:], rsrow.h[:], 1e-5, None, ALU.add),
          r=[rsrow.keys()], w=[rsrow.keys()])
        A("scalar", lambda e: e.activation(rsrow.h[:], rsrow.h[:], ACT.Ln),
          r=[rsrow.keys()], w=[rsrow.keys()])
        A("scalar", lambda e: e.activation(rsrow.h[:], rsrow.h[:], ACT.Exp, scale=-0.5),
          r=[rsrow.keys()], w=[rsrow.keys()])
        A("tensor", lambda e: e.matmul(PBC.h[:], ones.h[0:1, 0:128], murow.h[:],
                                       start=True, stop=True),
          r=[ones.keys(), murow.keys()], w=[PBC.keys()])
        A("tensor", lambda e: e.matmul(PM.h[:], ones.h[0:1, 0:128], rsrow.h[:],
                                       start=True, stop=True),
          r=[ones.keys(), rsrow.keys()], w=[PM.keys()])
        for oc in range(EC):
            r3buf = pTA if oc < 4 else pTB
            rc0 = (oc % 4) * S + c0
            A("vector", (lambda r3buf=r3buf, rc0=rc0: lambda e: e.tensor_tensor(
                sD_a.h[:], r3buf.h[:, rc0:rc0 + CH], PBC.h[:], ALU.subtract))(),
              r=[r3buf.keys(rc0, rc0 + CH), PBC.keys()], w=[sD_a.keys()])
            A("vector", lambda e: e.tensor_tensor(sD_b.h[:], sD_a.h[:], PM.h[:], ALU.mult),
              r=[sD_a.keys(), PM.keys()], w=[sD_b.keys()])
            A("sync", (lambda s=s, oc=oc, c0=c0, c1=c1: lambda e: e.dma_start(
                out=outT[s, oc * 128:(oc + 1) * 128, c0:c1], in_=sD_b.h[:]))(),
              r=[sD_b.keys()], w=[(("out", (s, oc, tc)),)], dma="dout")

    def phaseD(s):
        A("gpsimd", wffn_dma, w=[vTa.keys(0, EC * 1024), (("vTa_ones", 0),)], dma="dwf")
        setA = {"t": sc_t, "c1": sc_c1, "n1": sc_n1, "w": sc_w, "sq": sc_sq, "ff": sc_ff}
        setB = {"t": sB_t, "c1": sB_c1, "n1": sB_n1, "w": sB_w, "sq": sB_sq, "ff": sB_ff}
        setC = {"t": sC_t, "c1": sC_c1, "n1": sC_n1, "w": sC_w, "sq": sC_sq, "ff": sC_ff}
        SETS = (setA, setB, setC)
        for tc in range(TC):
            ln_tc(s, "r2", tc)
            groups = ((0, 1, 2), (3, 4, 5), (6, 7))
            d3q = [(2 * g + o) for g in range(3) for o in range(3) if 2 * g + o < 8]
            nd3 = 0
            for gi, grp in enumerate(groups):
                streams = [chain_ops(s, tc, ec, SETS[j], ec % 4, ci=j)
                           for j, ec in enumerate(grp)]
                for i in range(max(len(o) for o in streams)):
                    for ops in streams:
                        if i < len(ops):
                            eng, fn, r, w, dma = ops[i]
                            A(eng, fn, r=r, w=w, dma=dma)
                if tc >= 1:
                    take = 3 if gi < 2 else 2
                    for _ in range(take):
                        d3_group(s, tc - 1, nd3)
                        nd3 += 1
            if tc >= 1:
                d4_block(s, tc - 1)
        for pp in range(4):
            d3_group(s, TC - 1, 2 * pp)
            d3_group(s, TC - 1, 2 * pp + 1)
        d4_block(s, TC - 1)

    for s in range(B):
        phaseA(s)
        phaseB(s)
        A("vector", lambda e: e.memset(ka.h[64:65, :], -1.0),
          w=[ka.keys(0, 2 * S), (("ka_neg1", 0),)])
        def wqk_load(p):
            nb = p % 2
            A("gpsimd", wqk_dma(wqk[nb], p, 0), w=[(("wqkh", nb, 0),)],
              dma="dwq00" if nb == 0 else "dwq10")
            A("gpsimd", wqk_dma(wqk[nb], p, 1), w=[(("wqkh", nb, 1),)],
              dma="dwq01" if nb == 0 else "dwq11")
        wqk_load(0)
        for p in range(H // 2):
            if p + 1 < H // 2:
                wqk_load(p + 1)
            phaseB2_pair(s, p, p % 2)
            stage_dmas(s, p)
            stage_norms(s, p)
            phaseC_pair(s, p)
        phaseD(s)

    counts, per_eng = sch.finalize()

    with es:
        sems = {c: es.enter_context(nc.semaphore("c_" + c)) for c in CHANNELS}
        with nc.Block() as block:
            def emit_stream(engobj, eng):
                watermark = {}
                for op in per_eng[eng]:
                    for chan, prod in op.waits.items():
                        val = prod.tok
                        if val is None:
                            raise RuntimeError(f"unassigned token (tag={prod.tag})")
                        if watermark.get(chan, -1) < val:
                            engobj.wait_ge(sems[chan], val)
                            watermark[chan] = val
                    instr = op.fn(engobj)
                    if op.needs_inc:
                        if instr is None:
                            raise RuntimeError(f"op returned no instruction (tag={op.tag})")
                        instr.then_inc(sems[op.chan], op.inc_k)

            @block.sync
            def _(sync):
                emit_stream(sync, "sync")
                if counts["dout"]:
                    sync.wait_ge(sems["dout"], counts["dout"])

            @block.gpsimd
            def _(g):
                emit_stream(g, "gpsimd")

            @block.vector
            def _(v):
                emit_stream(v, "vector")

            @block.scalar
            def _(s_):
                emit_stream(s_, "scalar")

            @block.tensor
            def _(t):
                emit_stream(t, "tensor")

    return nc


_NC_CACHE = [None]


def _host_stage(L0_attn_x, L0_attn_gamma, L0_attn_beta, L0_attn_w_qkv,
                L0_ffn_weight_fp8, L0_ffn_bias, L0_ffn_input_scale, L0_ffn_weight_scale):
    x = np.ascontiguousarray(np.asarray(L0_attn_x, dtype=np.float32))
    wqkvT = np.ascontiguousarray(np.asarray(L0_attn_w_qkv, dtype=np.float32).T)
    wffnT = np.ascontiguousarray(np.asarray(L0_ffn_weight_fp8, dtype=np.float32).T)
    gcol = np.ascontiguousarray(np.asarray(L0_attn_gamma, np.float32).reshape(EC, 128).T)
    bcol = np.ascontiguousarray(np.asarray(L0_attn_beta, np.float32).reshape(EC, 128).T)
    fbcol = np.ascontiguousarray(np.asarray(L0_ffn_bias, np.float32).reshape(EC, 128).T)
    iscale = np.ascontiguousarray(np.asarray(L0_ffn_input_scale, np.float32).reshape(1, 1))
    wscale = np.ascontiguousarray(np.asarray(L0_ffn_weight_scale, np.float32).reshape(1, 1))
    kl = np.arange(128)[:, None]
    ql = np.arange(128)[None, :]
    trimin = np.where(kl <= ql, np.float32(1e38), np.float32(-1e30)).astype(np.float32)
    in_maps = []
    for c in range(NCORES):
        xc = np.ascontiguousarray(x[2 * c:2 * c + 2].transpose(0, 2, 1))
        in_maps.append({
            "xT": xc, "wqkvT": wqkvT, "wffnT": wffnT,
            "gcol": gcol, "bcol": bcol, "fbcol": fbcol,
            "iscale": iscale, "wscale": wscale, "trimin": trimin,
        })
    return in_maps


def kernel(L0_attn_x, L0_attn_gamma, L0_attn_beta, L0_attn_w_qkv,
           L0_ffn_weight_fp8, L0_ffn_bias, L0_ffn_input_scale, L0_ffn_weight_scale):
    in_maps = _host_stage(L0_attn_x, L0_attn_gamma, L0_attn_beta, L0_attn_w_qkv,
                          L0_ffn_weight_fp8, L0_ffn_bias, L0_ffn_input_scale,
                          L0_ffn_weight_scale)
    if _NC_CACHE[0] is None:
        _NC_CACHE[0] = build_nc()
    nc = _NC_CACHE[0]
    res = run_bass_kernel_spmd(nc, in_maps, core_ids=list(range(NCORES)))
    outs = []
    for c in range(NCORES):
        o = np.asarray(res.results[c]["outT"])
        outs.append(np.ascontiguousarray(o.transpose(0, 2, 1)))
    return np.concatenate(outs, axis=0).astype(np.float32)
